# version 01
# baseline (speedup 1.0000x reference)
import numpy as np
import ml_dtypes
import concourse.bass as bass
import concourse.mybir as mybir
from concourse.bass_utils import run_bass_kernel_spmd

F32 = mybir.dt.float32
BF16 = mybir.dt.bfloat16
F8 = mybir.dt.float8e4
I32 = mybir.dt.int32
AF = mybir.ActivationFunctionType
ALU = mybir.AluOpType
AXL = mybir.AxisListType

NCORES = 8
B = 2
L = 8192
DM = 1024
T = 2048
MAGIC = 1.5 * 2.0 ** 23

DSSM = 2048
DST = 128
NH0 = 32
HD0 = 64
CONV_DIM = DSSM + 2 * DST
D_IN_PROJ = 2 * DSSM + 2 * DST + NH0
CH0 = 256
NCH0 = T // CH0

H1 = 8
KD1 = 128
VD1 = 256
KEY_DIM = H1 * KD1
VAL_DIM = H1 * VD1
CH1 = 64
NCH1 = T // CH1

bf16 = ml_dtypes.bfloat16


def cdiv(a, b):
    return (a + b - 1) // b


class Ctx:

    ENGS = ('v', 's', 't', 'g', 'y')

    def __init__(self, nc):
        self.nc = nc
        self.sem = {e: nc.alloc_semaphore('ctr_' + e) for e in self.ENGS}
        self.cnt = {e: 0 for e in self.ENGS}
        self.script = {e: [] for e in self.ENGS}
        self.dsem = {}
        self.dcnt = {}

    def _dma_sem(self, name):
        if name not in self.dsem:
            self.dsem[name] = self.nc.alloc_semaphore('dma_' + name)
            self.dcnt[name] = 0
        return self.dsem[name]

    def _wrap(self, eng_key, emit, waits, inc):
        def fn(eng):
            for (sem, val) in waits:
                eng.wait_ge(sem, val)
            instr = emit(eng)
            if instr is not None and inc:
                instr.then_inc(self.sem[eng_key], 1)
        return fn

    def op(self, e, emit, waits=(), inc=True):
        wl = [w for w in waits if w is not None]
        self.script[e].append(self._wrap(e, emit, wl, inc))
        if inc:
            self.cnt[e] += 1
        return (self.sem[e], self.cnt[e])

    def dma(self, e, sem_name, out_ap, in_ap, waits=(), serial=False):
        sem = self._dma_sem(sem_name)
        wl = [w for w in waits if w is not None]
        if serial and self.dcnt[sem_name] > 0:
            wl = wl + [(sem, self.dcnt[sem_name])]
        def emit(eng):
            eng.dma_start(out=out_ap, in_=in_ap).then_inc(sem, 16)
            return None
        self.script[e].append(self._wrap(e, emit, wl, False))
        self.dcnt[sem_name] += 16
        return (sem, self.dcnt[sem_name])

    def dtok(self, name):
        return (self.dsem[name], self.dcnt[name])

    def tok(self, e):
        return (self.sem[e], self.cnt[e])

    def finish(self):
        for name in self.dsem:
            sem, cnt = self.dsem[name], self.dcnt[name]
            self.script['y'].append(lambda eng, s=sem, c=cnt: eng.wait_ge(s, c))

    def const_col(self, value, name=None):
        t = self.nc.alloc_sbuf_tensor(name or ('c_%d' % len(self.script['v'])), [128, 1], F32)
        tok = self.op('v', lambda v: v.memset(t[:], value))
        return t, tok

    def emit(self):
        self.finish()
        with self.nc.Block() as block:
            for e, attr in (('y', 'sync'), ('g', 'gpsimd'), ('v', 'vector'),
                            ('s', 'scalar'), ('t', 'tensor')):
                fns = self.script[e]
                if not fns:
                    continue
                def body(eng, fns=fns):
                    for fn in fns:
                        fn(eng)
                getattr(block, attr)(body)



def build_linear(K, OD, Tn, bias=False, residual=False, wscale=False,
                 out_dtype=F32, x_hbm_dtype=F32, res_dtype=F32,
                 stage_dtype=None, post_od=None, residual2=False):
    nc = bass.Bass('TRN2', target_bir_lowering=False)
    x = nc.dram_tensor('x', [K, Tn], x_hbm_dtype, kind='ExternalInput').ap()
    w = nc.dram_tensor('w', [K, OD], F32, kind='ExternalInput').ap()
    b_ap = nc.dram_tensor('bias', [OD, 1], F32, kind='ExternalInput').ap() if bias else None
    r_ap = nc.dram_tensor('resid', [OD, Tn], res_dtype, kind='ExternalInput').ap() if residual else None
    ws_ap = nc.dram_tensor('wscale', [K, 1], F32, kind='ExternalInput').ap() if wscale else None
    r2_ap = nc.dram_tensor('resid2', [OD, Tn], F32, kind='ExternalInput').ap() if residual2 else None
    out = nc.dram_tensor('out', [OD, Tn], out_dtype, kind='ExternalOutput').ap()

    nk = K // 128
    od_tiles = []
    od0 = 0
    while od0 < OD:
        odp = min(128, OD - od0)
        od_tiles.append((od0, odp))
        od0 += odp
    TOKBLK = 512
    ntok = cdiv(Tn, TOKBLK)

    sx = nc.alloc_sbuf_tensor('sx', [128, nk * Tn], BF16)
    sw = nc.alloc_sbuf_tensor('sw', [128, nk * OD], BF16)
    sbias = nc.alloc_sbuf_tensor('sbias', [128, len(od_tiles)], F32) if bias else None
    sws = nc.alloc_sbuf_tensor('sws', [128, nk], F32) if wscale else None
    sres = nc.alloc_sbuf_tensor('sres', [128, nk_res * Tn], res_dtype) \
        if residual and (nk_res := len(od_tiles)) else None
    sres2 = nc.alloc_sbuf_tensor('sres2', [128, 2 * Tn], F32) if residual2 else None
    sout = nc.alloc_sbuf_tensor('sout', [128, 2 * Tn], stage_dtype or out_dtype)
    pz = [nc.alloc_psum_tensor('pz%d' % i, [128, TOKBLK], F32) for i in range(2)]

    ctx = Ctx(nc)

    for k in range(nk):
        ctx.dma('g', 'ld', sx[:, k * Tn:(k + 1) * Tn], x[k * 128:(k + 1) * 128, :])
    for k in range(nk):
        ctx.dma('g', 'ld', sw[:, k * OD:(k + 1) * OD], w[k * 128:(k + 1) * 128, :])
    if bias:
        for i, (o0, op_) in enumerate(od_tiles):
            ctx.dma('y', 'ldh', sbias[0:op_, i:i + 1], b_ap[o0:o0 + op_, :])
    if wscale:
        for k in range(nk):
            ctx.dma('y', 'ldh', sws[:, k:k + 1], ws_ap[k * 128:(k + 1) * 128, :])
    if residual:
        for i, (o0, op_) in enumerate(od_tiles):
            ctx.dma('y', 'ldh', sres[0:op_, i * Tn:(i + 1) * Tn], r_ap[o0:o0 + op_, :])

    gl = ctx.dtok('ld')
    hl = ctx.dtok('ldh') if 'ldh' in ctx.dsem else None

    wtok = None
    if wscale:
        for k in range(nk):
            wtok = ctx.op('v', lambda v, k=k: v.tensor_scalar(
                sw[:, k * OD:(k + 1) * OD], sw[:, k * OD:(k + 1) * OD],
                sws[:, k:k + 1], None, ALU.mult), waits=[gl, hl])

    prev_evac = {}
    store_cnt = {0: 0, 1: 0}
    for i, (o0, odp) in enumerate(od_tiles):
        par = i % 2
        ot = sout[0:odp, par * Tn:par * Tn + Tn]
        last_ev = None
        for j in range(ntok):
            t0 = j * TOKBLK
            tn = min(TOKBLK, Tn - t0)
            bank = (i * ntok + j) % 2
            ps = pz[bank]
            for k in range(nk):
                def mm(te, k=k, o0=o0, odp=odp, t0=t0, tn=tn, ps=ps):
                    return te.matmul(ps[0:odp, 0:tn],
                                     sw[:, k * OD + o0:k * OD + o0 + odp],
                                     sx[:, k * Tn + t0:k * Tn + t0 + tn],
                                     start=(k == 0), stop=(k == nk - 1))
                w8 = [gl, wtok, prev_evac.get(bank)] if k == 0 else []
                tk = ctx.op('t', mm, waits=w8)
            if False:
                pass
            else:
                def ev_fn(v, i=i, odp=odp, t0=t0, tn=tn, ps=ps, ot=ot):
                    if residual:
                        return v.scalar_tensor_tensor(
                            ot[:, t0:t0 + tn], ps[0:odp, 0:tn],
                            sbias[0:odp, i:i + 1] if bias else 0.0,
                            sres[0:odp, i * Tn + t0:i * Tn + t0 + tn],
                            ALU.add, ALU.add)
                    elif bias:
                        return v.tensor_scalar(ot[:, t0:t0 + tn], ps[0:odp, 0:tn],
                                               sbias[0:odp, i:i + 1], None, ALU.add)
                    else:
                        return v.tensor_copy(ot[:, t0:t0 + tn], ps[0:odp, 0:tn])
                wpar = ctx.dtok('st%d' % par) if ('st%d' % par) in ctx.dsem else None
                ev = ctx.op('v', ev_fn, waits=[tk, hl, wpar])
            prev_evac[bank] = ev
            last_ev = ev
        if post_od is not None:
            ctx._sres2 = sres2
            ctx._r2ap = r2_ap
            ctx._hl = hl
            src_ap, last_ev = post_od(nc, ctx, i, o0, odp, ot, par, last_ev)
        else:
            src_ap = ot
        ctx.dma('y', 'st%d' % par, out[o0:o0 + odp, :], src_ap, waits=[last_ev],
                serial=True)

    ctx.emit()
    return nc



def build_ln(R, D, eps=1e-5, out_dtype=F32):
    nc = bass.Bass('TRN2', target_bir_lowering=False)
    x = nc.dram_tensor('x', [R, D], F32, kind='ExternalInput').ap()
    out = nc.dram_tensor('out', [R, D], out_dtype, kind='ExternalOutput').ap()
    nr = R // 128
    sx = nc.alloc_sbuf_tensor('sx', [128, 2 * D], F32)
    sq = nc.alloc_sbuf_tensor('sq', [128, 2 * D], F32)
    sm = nc.alloc_sbuf_tensor('sm', [128, 2 * 8], F32)
    so = nc.alloc_sbuf_tensor('so', [128, 2 * D], out_dtype)
    ctx = Ctx(nc)
    eps_ap, eps_tok = ctx.const_col(eps, 'eps')
    prev_use = {}
    for r in range(nr):
        par = r % 2
        xt = sx[:, par * D:(par + 1) * D]
        qt = sq[:, par * D:(par + 1) * D]
        mt = sm[:, par * 8:(par + 1) * 8]
        otile = so[:, par * D:(par + 1) * D]
        lt = ctx.dma('y', 'x%d' % par, xt, x[r * 128:(r + 1) * 128, :],
                     waits=[prev_use.get(par)], serial=True)
        t1 = ctx.op('v', lambda v, xt=xt, mt=mt: v.reduce_sum(mt[:, 0:1], xt, axis=AXL.X),
                    waits=[lt])
        t2 = ctx.op('s', lambda s, xt=xt, qt=qt, mt=mt: s.activation(
            qt, xt, AF.Square, accum_out=mt[:, 1:2]), waits=[lt])
        t3 = ctx.op('v', lambda v, mt=mt: v.tensor_scalar(mt[:, 0:1], mt[:, 0:1],
                                                          1.0 / D, None, ALU.mult),
                    waits=[t1])
        t4 = ctx.op('v', lambda v, mt=mt: v.tensor_tensor(mt[:, 2:3], mt[:, 0:1],
                                                          mt[:, 0:1], ALU.mult), waits=[t3])
        t5 = ctx.op('v', lambda v, mt=mt: v.tensor_scalar(mt[:, 1:2], mt[:, 1:2],
                                                          1.0 / D, None, ALU.mult),
                    waits=[t2])
        t6 = ctx.op('v', lambda v, mt=mt: v.tensor_tensor(mt[:, 3:4], mt[:, 1:2],
                                                          mt[:, 2:3], ALU.subtract),
                    waits=[t4, t5])
        t7 = ctx.op('s', lambda s, mt=mt: s.activation(mt[:, 4:5], mt[:, 3:4], AF.Sqrt,
                                                       bias=eps_ap[:, 0:1]), waits=[t6, eps_tok])
        t8 = ctx.op('v', lambda v, mt=mt: v.reciprocal(mt[:, 5:6], mt[:, 4:5]), waits=[t7])
        stw = ctx.dtok('o%d' % par) if ('o%d' % par) in ctx.dsem else None
        t9 = ctx.op('v', lambda v, xt=xt, otile=otile, mt=mt: v.tensor_scalar(
            otile, xt, mt[:, 0:1], mt[:, 5:6], ALU.subtract, ALU.mult),
            waits=[t8, stw])
        prev_use[par] = t9
        ctx.dma('y', 'o%d' % par, out[r * 128:(r + 1) * 128, :], otile,
                waits=[t9], serial=True)
    ctx.emit()
    return nc



def build_mamba_pre(Tn=T, nch=None, ch=CH0):
    if nch is None:
        nch = Tn // ch
    nc = bass.Bass('TRN2', target_bir_lowering=False)
    xbc = nc.dram_tensor('xbc_pad', [CONV_DIM, Tn + 3], F32, kind='ExternalInput').ap()
    cw = nc.dram_tensor('conv_w', [CONV_DIM, 4], F32, kind='ExternalInput').ap()
    cb = nc.dram_tensor('conv_b', [CONV_DIM, 1], F32, kind='ExternalInput').ap()
    dtr = nc.dram_tensor('dt_raw', [NH0, Tn], F32, kind='ExternalInput').ap()
    dtb = nc.dram_tensor('dt_bias', [NH0, 1], F32, kind='ExternalInput').ap()
    alog = nc.dram_tensor('A_log', [NH0, 1], F32, kind='ExternalInput').ap()
    convout = nc.dram_tensor('convout', [CONV_DIM, Tn], F32, kind='ExternalOutput').ap()
    dt_o = nc.dram_tensor('dt', [NH0, Tn], F32, kind='ExternalOutput').ap()
    cs_o = nc.dram_tensor('cs', [NH0, Tn], F32, kind='ExternalOutput').ap()
    expcs_o = nc.dram_tensor('expcs', [NH0, Tn], F32, kind='ExternalOutput').ap()
    dsdt_o = nc.dram_tensor('dsdt', [NH0, Tn], F32, kind='ExternalOutput').ap()

    nct = CONV_DIM // 128
    sx = nc.alloc_sbuf_tensor('sx', [128, 2 * (Tn + 3)], F32)
    swc = nc.alloc_sbuf_tensor('swc', [128, nct * 4], F32)
    sbc = nc.alloc_sbuf_tensor('sbc', [128, nct], F32)
    sacc = nc.alloc_sbuf_tensor('sacc', [128, 2 * Tn], F32)
    sco = nc.alloc_sbuf_tensor('sco', [128, 2 * Tn], F32)
    sdt = nc.alloc_sbuf_tensor('sdt', [NH0, 8 * Tn], F32)
    scol = nc.alloc_sbuf_tensor('scol', [NH0, 4], F32)

    ctx = Ctx(nc)
    one_ap, one_tok = ctx.const_col(1.0, 'one')

    U = sdt[:, 0 * Tn:1 * Tn]
    AB = sdt[:, 1 * Tn:2 * Tn]
    DT = sdt[:, 2 * Tn:3 * Tn]
    CS = sdt[:, 3 * Tn:4 * Tn]
    EC = sdt[:, 4 * Tn:5 * Tn]
    DS = sdt[:, 5 * Tn:6 * Tn]
    TMP = sdt[:, 6 * Tn:7 * Tn]
    ZERO = sdt[:, 7 * Tn:8 * Tn]

    for i in range(nct):
        ctx.dma('y', 'cw', swc[:, 4 * i:4 * i + 4], cw[i * 128:(i + 1) * 128, :])
        ctx.dma('y', 'cw', sbc[:, i:i + 1], cb[i * 128:(i + 1) * 128, :])
    ctx.dma('y', 'dt', sdt[:, 0:Tn], dtr[:, :])
    ctx.dma('y', 'dt', scol[:, 0:1], dtb[:, :])
    ctx.dma('y', 'dt', scol[:, 1:2], alog[:, :])
    cwl = ctx.dtok('cw')
    dtl = ctx.dtok('dt')

    z0 = ctx.op('v', lambda v: v.memset(ZERO, 0.0))
    t = ctx.op('v', lambda v: v.tensor_scalar(U, U, scol[:, 0:1], None, ALU.add),
               waits=[dtl])
    t = ctx.op('s', lambda s: s.activation(AB, U, AF.Abs), waits=[t])
    t = ctx.op('s', lambda s: s.activation(AB, AB, AF.Exp, scale=-1.0), waits=[t])
    t = ctx.op('s', lambda s: s.activation(AB, AB, AF.Ln, bias=one_ap[0:NH0, 0:1]), waits=[t, one_tok])
    t2 = ctx.op('v', lambda v: v.tensor_scalar(TMP, U, 0.0, None, ALU.max), waits=[t])
    tdt = ctx.op('v', lambda v: v.tensor_tensor(DT, AB, TMP, ALU.add), waits=[t, t2])
    t = ctx.op('s', lambda s: s.activation(scol[:, 2:3], scol[:, 1:2], AF.Exp), waits=[dtl])
    t = ctx.op('v', lambda v: v.tensor_scalar(scol[:, 2:3], scol[:, 2:3], -1.0, None,
                                              ALU.mult), waits=[t])
    tda = ctx.op('v', lambda v: v.tensor_scalar(TMP, DT, scol[:, 2:3], None, ALU.mult),
                 waits=[t, tdt])
    tcs = None
    for c in range(nch):
        sl = slice(c * ch, (c + 1) * ch)
        tcs = ctx.op('v', lambda v, sl=sl: v.tensor_tensor_scan(
            CS[:, sl], TMP[:, sl], ZERO[:, sl], 0.0, ALU.add, ALU.add),
            waits=[tda, z0])
    tec = ctx.op('s', lambda s: s.activation(EC, CS, AF.Exp), waits=[tcs])
    tds = None
    for c in range(nch):
        sl = slice(c * ch, (c + 1) * ch)
        last = slice((c + 1) * ch - 1, (c + 1) * ch)
        tt = ctx.op('s', lambda s, sl=sl, last=last: s.activation(
            DS[:, sl], CS[:, sl], AF.Exp, scale=-1.0, bias=CS[:, last]), waits=[tcs])
        tds = tt
    tds2 = ctx.op('v', lambda v: v.tensor_tensor(DS, DS, DT, ALU.mult), waits=[tds, tdt])
    ctx.dma('y', 'odt', dt_o[:, :], DT, waits=[tdt])
    ctx.dma('y', 'ocs', cs_o[:, :], CS, waits=[tcs])
    ctx.dma('y', 'oec', expcs_o[:, :], EC, waits=[tec])
    ctx.dma('y', 'ods', dsdt_o[:, :], DS, waits=[tds2])

    prev_use = {}
    for i in range(nct):
        par = i % 2
        xt = sx[:, par * (Tn + 3):par * (Tn + 3) + Tn + 3]
        at = sacc[:, par * Tn:(par + 1) * Tn]
        ct = sco[:, par * Tn:(par + 1) * Tn]
        stw = ctx.dtok('co%d' % par) if ('co%d' % par) in ctx.dsem else None
        lt = ctx.dma('g', 'x%d' % par, xt, xbc[i * 128:(i + 1) * 128, :], serial=True,
                     waits=[prev_use.get(par)])
        t = ctx.op('v', lambda v, xt=xt, at=at, i=i: v.tensor_scalar(
            at, xt[:, 0:Tn], swc[:, 4 * i:4 * i + 1], sbc[:, i:i + 1],
            ALU.mult, ALU.add), waits=[lt, cwl, prev_use.get(('silu', par))])
        for k in range(1, 4):
            t = ctx.op('v', lambda v, xt=xt, at=at, i=i, k=k: v.scalar_tensor_tensor(
                at, xt[:, k:k + Tn], swc[:, 4 * i + k:4 * i + k + 1], at,
                ALU.mult, ALU.add), waits=[t])
        prev_use[par] = t
        t = ctx.op('s', lambda s, at=at, ct=ct: s.activation(ct, at, AF.Silu),
                   waits=[t, stw])
        prev_use[('silu', par)] = t
        ctx.dma('y', 'co%d' % par, convout[i * 128:(i + 1) * 128, :], ct,
                waits=[t], serial=True)

    ctx.emit()
    return nc



def build_ssd_local(Tn=T, ch=CH0):
    nch = Tn // ch
    njt = ch // 128
    ncx = DSSM // 128
    nc = bass.Bass('TRN2', target_bir_lowering=False)
    x_cm = nc.dram_tensor('x_cm', [DSSM, Tn], F32, kind='ExternalInput').ap()
    b_cm = nc.dram_tensor('B_cm', [DST, Tn], F32, kind='ExternalInput').ap()
    c_cm = nc.dram_tensor('C_cm', [DST, Tn], F32, kind='ExternalInput').ap()
    dt_tm = nc.dram_tensor('dt_tm', [Tn, NH0], F32, kind='ExternalInput').ap()
    cs_row = nc.dram_tensor('cs_row', [NH0, Tn], F32, kind='ExternalInput').ap()
    cs_tm = nc.dram_tensor('cs_tm', [Tn, NH0], F32, kind='ExternalInput').ap()
    dsdt_tm = nc.dram_tensor('dsdt_tm', [Tn, NH0], F32, kind='ExternalInput').ap()
    d_bc = nc.dram_tensor('D_bc', [128, NH0], F32, kind='ExternalInput').ap()
    mask_h = nc.dram_tensor('mask', [ch, ch], F32, kind='ExternalInput').ap()
    ones1_h = nc.dram_tensor('ones1', [1, 128], F32, kind='ExternalInput').ap()
    ident_h = nc.dram_tensor('ident', [128, 128], BF16, kind='ExternalInput').ap()
    ydx_o = nc.dram_tensor('ydx', [Tn, DSSM], F32, kind='ExternalOutput').ap()
    st_o = nc.dram_tensor('states', [nch * DSSM, DST], F32, kind='ExternalOutput').ap()

    sxu = nc.alloc_sbuf_tensor('sxu', [128, ncx * ch], BF16)
    sxt = nc.alloc_sbuf_tensor('sxt', [128, njt * DSSM], BF16)
    sbu = nc.alloc_sbuf_tensor('sbu', [128, ch], BF16)
    sbt = nc.alloc_sbuf_tensor('sbt', [128, njt * DST], BF16)
    scu = nc.alloc_sbuf_tensor('scu', [128, ch], BF16)
    s_dt = nc.alloc_sbuf_tensor('s_dt', [128, njt * NH0], F32)
    s_cst = nc.alloc_sbuf_tensor('s_cst', [128, njt * NH0], F32)
    s_dsd = nc.alloc_sbuf_tensor('s_dsd', [128, njt * NH0], F32)
    s_csr = nc.alloc_sbuf_tensor('s_csr', [1, NH0 * ch], F32)
    s_db = nc.alloc_sbuf_tensor('s_db', [128, NH0], F32)
    s_mask = nc.alloc_sbuf_tensor('s_mask', [128, njt * ch], F32)
    s_ones = nc.alloc_sbuf_tensor('s_ones', [1, 128], F32)
    s_id = nc.alloc_sbuf_tensor('s_id', [128, 128], BF16)
    s_sm = nc.alloc_sbuf_tensor('s_sm', [128, njt * ch], BF16)
    s_csb = nc.alloc_sbuf_tensor('s_csb', [128, ch], F32)
    s_g = nc.alloc_sbuf_tensor('s_g', [128, njt * ch], F32)
    s_e = nc.alloc_sbuf_tensor('s_e', [128, njt * ch], BF16)
    s_m = nc.alloc_sbuf_tensor('s_m', [128, njt * ch], BF16)
    s_xdt = nc.alloc_sbuf_tensor('s_xdt', [128, njt * HD0], BF16)
    s_bt2 = nc.alloc_sbuf_tensor('s_bt2', [128, njt * DST], BF16)
    s_y = nc.alloc_sbuf_tensor('s_y', [128, njt * DSSM], F32)
    s_st = nc.alloc_sbuf_tensor('s_st', [128, ncx * DST], F32)

    p_tr = [nc.alloc_psum_tensor('ptr%d' % i, [128, 128], BF16) for i in range(2)]
    p_s = [nc.alloc_psum_tensor('p_s%d' % i, [128, ch], F32) for i in range(njt)]
    p_bc = nc.alloc_psum_tensor('p_bc', [128, ch], F32)
    p_y = [nc.alloc_psum_tensor('p_y%d' % i, [128, HD0], F32) for i in range(njt)]
    p_st = nc.alloc_psum_tensor('p_st', [64, DST], F32)

    ctx = Ctx(nc)

    ctx.dma('y', 'const', s_db[:, :], d_bc[:, :])
    for jt in range(njt):
        ctx.dma('y', 'const', s_mask[:, jt * ch:(jt + 1) * ch],
                mask_h[jt * 128:(jt + 1) * 128, :])
    ctx.dma('y', 'const', s_ones[:, :], ones1_h[:, :])
    ctx.dma('y', 'const', s_id[:, :], ident_h[:, :])
    ctok = ctx.dtok('const')

    for u in range(nch):
        t0 = u * ch
        ub = [ctx.tok('v'), ctx.tok('t'), ctx.tok('s'),
              ctx.dtok('oy') if 'oy' in ctx.dsem else None,
              ctx.dtok('ost') if 'ost' in ctx.dsem else None]
        for i in range(ncx):
            ctx.dma('g', 'ld', sxu[:, i * ch:(i + 1) * ch],
                    x_cm[i * 128:(i + 1) * 128, t0:t0 + ch], waits=ub, serial=True)
        ctx.dma('g', 'ld', sbu[:, :], b_cm[:, t0:t0 + ch], waits=ub, serial=True)
        ctx.dma('g', 'ld', scu[:, :], c_cm[:, t0:t0 + ch], waits=ub, serial=True)
        for jt in range(njt):
            r0 = t0 + jt * 128
            ctx.dma('y', 'ldh', s_dt[:, jt * NH0:(jt + 1) * NH0], dt_tm[r0:r0 + 128, :],
                    waits=ub, serial=True)
            ctx.dma('y', 'ldh', s_cst[:, jt * NH0:(jt + 1) * NH0], cs_tm[r0:r0 + 128, :],
                    waits=ub, serial=True)
            ctx.dma('y', 'ldh', s_dsd[:, jt * NH0:(jt + 1) * NH0], dsdt_tm[r0:r0 + 128, :],
                    waits=ub, serial=True)
        ctx.dma('y', 'ldh', s_csr[:, :].rearrange('o (h w) -> o h w', w=ch),
                cs_row[:, t0:t0 + ch].unsqueeze(0), waits=ub, serial=True)
        ld = ctx.dtok('ld')
        ldh = ctx.dtok('ldh')

        prev_ptr = {0: None, 1: None}
        for idx in range(ncx * njt + njt):
            if idx < ncx * njt:
                i, jt = divmod(idx, njt)
                src_ap = sxu[:, i * ch + jt * 128:i * ch + (jt + 1) * 128]
                dst_ap = sxt[:, jt * DSSM + i * 128:jt * DSSM + (i + 1) * 128]
            else:
                jt = idx - ncx * njt
                src_ap = sbu[:, jt * 128:(jt + 1) * 128]
                dst_ap = sbt[:, jt * DST:(jt + 1) * DST]
            pk = idx % 2
            tk = ctx.op('t', lambda te, s=src_ap, pk=pk: te.transpose(
                p_tr[pk][:, :], s, s_id[:, :]), waits=[ld, ctok, prev_ptr[pk]])
            ev = ctx.op('v', lambda v, d=dst_ap, pk=pk: v.tensor_copy(d, p_tr[pk][:, :]),
                        waits=[tk])
            prev_ptr[pk] = ev
        ttr = ctx.tok('v')

        tsm = []
        for jt in range(njt):
            tk = ctx.op('t', lambda te, jt=jt: te.matmul(
                p_s[jt][:, :], sbu[:, jt * 128:(jt + 1) * 128], scu[:, :],
                start=True, stop=True), waits=[ld, ub[0]])
            ev = ctx.op('v', lambda v, jt=jt: v.scalar_tensor_tensor(
                s_sm[:, jt * ch:(jt + 1) * ch], p_s[jt][:, :], 0.0,
                s_mask[:, jt * ch:(jt + 1) * ch], ALU.add, ALU.mult),
                waits=[tk, ctok])
            tsm.append(ev)

        prev = {}
        for h in range(NH0):
            tbc = ctx.op('t', lambda te, h=h: te.matmul(
                p_bc[:, :], s_ones[:, :], s_csr[0:1, h * ch:(h + 1) * ch],
                start=True, stop=True),
                waits=[ldh, ctok, prev.get('csb')])
            ebc = ctx.op('v', lambda v: v.tensor_copy(s_csb[:, :], p_bc[:, :]),
                         waits=[tbc, prev.get('csb_use')])
            tg = []
            for jt in range(njt):
                t1 = ctx.op('v', lambda v, jt=jt, h=h: v.tensor_scalar(
                    s_g[:, jt * ch:(jt + 1) * ch], s_csb[:, :],
                    s_cst[:, jt * NH0 + h:jt * NH0 + h + 1], 0.0,
                    ALU.subtract, ALU.min), waits=[ebc, ldh, prev.get(('g', jt))])
                t2 = ctx.op('s', lambda s, jt=jt: s.activation(
                    s_e[:, jt * ch:(jt + 1) * ch], s_g[:, jt * ch:(jt + 1) * ch], AF.Exp),
                    waits=[t1, prev.get(('e', jt))])
                prev[('g', jt)] = t2
                t3 = ctx.op('v', lambda v, jt=jt: v.tensor_tensor(
                    s_m[:, jt * ch:(jt + 1) * ch], s_e[:, jt * ch:(jt + 1) * ch],
                    s_sm[:, jt * ch:(jt + 1) * ch], ALU.mult),
                    waits=[t2, tsm[jt], prev.get(('m', jt))])
                prev[('e', jt)] = t3
                tg.append(t3)
            prev['csb_use'] = tg[0]
            prev['csb'] = tg[njt - 1]
            txd = []
            for jt in range(njt):
                t4 = ctx.op('v', lambda v, jt=jt, h=h: v.tensor_scalar(
                    s_xdt[:, jt * HD0:(jt + 1) * HD0],
                    sxt[:, jt * DSSM + h * HD0:jt * DSSM + (h + 1) * HD0],
                    s_dt[:, jt * NH0 + h:jt * NH0 + h + 1], None, ALU.mult),
                    waits=[ttr, ldh, prev.get(('xdt', jt))])
                t5 = ctx.op('v', lambda v, jt=jt, h=h: v.tensor_scalar(
                    s_bt2[:, jt * DST:(jt + 1) * DST], sbt[:, jt * DST:(jt + 1) * DST],
                    s_dsd[:, jt * NH0 + h:jt * NH0 + h + 1], None, ALU.mult),
                    waits=[ttr, ldh, prev.get(('bt2', jt))])
                txd.append((t4, t5))
            tmy = []
            for it in range(njt):
                tk = None
                for jt in range(njt):
                    tk = ctx.op('t', lambda te, it=it, jt=jt: te.matmul(
                        p_y[it][:, :],
                        s_m[:, jt * ch + it * 128:jt * ch + (it + 1) * 128],
                        s_xdt[:, jt * HD0:(jt + 1) * HD0],
                        start=(jt == 0), stop=(jt == njt - 1)),
                        waits=[tg[jt], txd[jt][0],
                               prev.get(('py', it)) if jt == 0 else None])
                tmy.append(tk)
            tst = None
            for jt in range(njt):
                tst = ctx.op('t', lambda te, jt=jt, h=h: te.matmul(
                    p_st[0:64, :], sxt[:, jt * DSSM + h * HD0:jt * DSSM + (h + 1) * HD0],
                    s_bt2[:, jt * DST:(jt + 1) * DST],
                    start=(jt == 0), stop=(jt == njt - 1)),
                    waits=[txd[jt][0], txd[jt][1],
                           prev.get('pst') if jt == 0 else None])
            for jt in range(njt):
                prev[('m', jt)] = tmy[njt - 1]
                prev[('xdt', jt)] = ctx.tok('t')
                prev[('bt2', jt)] = tst
            yevs = []
            for it in range(njt):
                ev = ctx.op('v', lambda v, it=it, h=h: v.scalar_tensor_tensor(
                    s_y[:, it * DSSM + h * HD0:it * DSSM + (h + 1) * HD0],
                    sxt[:, it * DSSM + h * HD0:it * DSSM + (h + 1) * HD0],
                    s_db[:, h:h + 1], p_y[it][:, :], ALU.mult, ALU.add),
                    waits=[tmy[it], ctok, ub[3]])
                prev[('py', it)] = ev
                yevs.append(ev)
            hb = (h % 2) * 64
            hc = (h // 2) * DST
            ev2 = ctx.op('v', lambda v, hb=hb, hc=hc: v.tensor_copy(
                s_st[hb:hb + 64, hc:hc + DST], p_st[0:64, :]),
                waits=[tst, ub[4]])
            prev['pst'] = ev2
        vtok = ctx.tok('v')
        for it in range(njt):
            ctx.dma('y', 'oy', ydx_o[t0 + it * 128:t0 + (it + 1) * 128, :],
                    s_y[:, it * DSSM:(it + 1) * DSSM], waits=[vtok], serial=True)
        ctx.dma('y', 'ost', st_o[u * DSSM:(u + 1) * DSSM, :].rearrange(
            '(a p) d -> p a d', p=128),
            s_st[:, :].rearrange('p (a d) -> p a d', d=DST),
            waits=[vtok], serial=True)

    ctx.emit()
    return nc



def build_state_scan(nchunks, rows, ndst=DST):
    nc = bass.Bass('TRN2', target_bir_lowering=False)
    st_h = nc.dram_tensor('states', [nchunks, rows, ndst], F32, kind='ExternalInput').ap()
    dc_h = nc.dram_tensor('decay_bc', [nchunks, 128, rows // 128], F32,
                          kind='ExternalInput').ap()
    pv_h = nc.dram_tensor('prev', [nchunks, rows, ndst], F32, kind='ExternalOutput').ap()
    nrt = rows // 128
    s_pv = nc.alloc_sbuf_tensor('s_pv', [128, nrt * ndst], F32)
    s_in = nc.alloc_sbuf_tensor('s_in', [128, 2 * nrt * ndst], F32)
    s_dc = nc.alloc_sbuf_tensor('s_dc', [128, 2 * nrt], F32)
    ctx = Ctx(nc)
    tz = ctx.op('v', lambda v: v.memset(s_pv[:, :], 0.0))
    upd = {i: tz for i in range(nrt)}
    for c in range(nchunks):
        par = c % 2
        ib = s_in[:, par * nrt * ndst:(par + 1) * nrt * ndst]
        db = s_dc[:, par * nrt:(par + 1) * nrt]
        lw = [upd[nrt - 1]] if c >= 2 else []
        lt = ctx.dma('g', 'ld%d' % par, ib.rearrange('p (a d) -> p a d', d=ndst),
                     st_h[c, :, :].rearrange('(a p) d -> p a d', p=128),
                     waits=lw, serial=True)
        lt2 = ctx.dma('g', 'ld%d' % par, db, dc_h[c, :, :], serial=True)
        stok = ctx.dma('y', 'st', pv_h[c, :, :].rearrange('(a p) d -> p a d', p=128),
                       s_pv[:, :].rearrange('p (a d) -> p a d', d=ndst),
                       waits=[upd[nrt - 1]], serial=True)
        for a in range(nrt):
            upd[a] = ctx.op('v', lambda v, a=a, ib=ib, db=db: v.scalar_tensor_tensor(
                s_pv[:, a * ndst:(a + 1) * ndst], s_pv[:, a * ndst:(a + 1) * ndst],
                db[:, a:a + 1], ib[:, a * ndst:(a + 1) * ndst], ALU.mult, ALU.add),
                waits=[lt2, stok, upd[a]])
    ctx.emit()
    return nc



def build_ssd_final(Tn=T, ch=CH0):
    nch = Tn // ch
    njt = ch // 128
    nc = bass.Bass('TRN2', target_bir_lowering=False)
    ydx_h = nc.dram_tensor('ydx', [Tn, DSSM], F32, kind='ExternalInput').ap()
    c_cm = nc.dram_tensor('C_cm', [DST, Tn], F32, kind='ExternalInput').ap()
    pvt_h = nc.dram_tensor('prevT', [nch * DST, DSSM], F32, kind='ExternalInput').ap()
    ecs_h = nc.dram_tensor('expcs_tm', [Tn, NH0], F32, kind='ExternalInput').ap()
    z_h = nc.dram_tensor('z_tm', [Tn, DSSM], F32, kind='ExternalInput').ap()
    yn_o = nc.dram_tensor('yn', [Tn, DSSM], BF16, kind='ExternalOutput').ap()

    s_y = nc.alloc_sbuf_tensor('s_y', [128, njt * DSSM], F32)
    s_cu = nc.alloc_sbuf_tensor('s_cu', [128, ch], BF16)
    s_pv = nc.alloc_sbuf_tensor('s_pv', [128, DSSM], BF16)
    s_ec = nc.alloc_sbuf_tensor('s_ec', [128, njt * NH0], F32)
    s_z = nc.alloc_sbuf_tensor('s_z', [128, njt * DSSM], F32)
    s_sq = nc.alloc_sbuf_tensor('s_sq', [128, DSSM], BF16)
    s_ms = nc.alloc_sbuf_tensor('s_ms', [128, 8], F32)
    s_o = nc.alloc_sbuf_tensor('s_o', [128, njt * DSSM], BF16)
    p_yo = [nc.alloc_psum_tensor('p_yo%d' % i, [128, HD0], F32) for i in range(njt)]

    ctx = Ctx(nc)
    eps_ap, eps_tok = ctx.const_col(1e-12, 'eps')

    for u in range(nch):
        t0 = u * ch
        ub = [ctx.tok('v'), ctx.tok('t'), ctx.tok('s'),
              ctx.dtok('oy') if 'oy' in ctx.dsem else None]
        for jt in range(njt):
            r0 = t0 + jt * 128
            ctx.dma('y', 'ld', s_y[:, jt * DSSM:(jt + 1) * DSSM], ydx_h[r0:r0 + 128, :],
                    waits=ub, serial=True)
            ctx.dma('y', 'ld', s_ec[:, jt * NH0:(jt + 1) * NH0], ecs_h[r0:r0 + 128, :],
                    waits=ub, serial=True)
            ctx.dma('y', 'ld', s_z[:, jt * DSSM:(jt + 1) * DSSM], z_h[r0:r0 + 128, :],
                    waits=ub, serial=True)
        ctx.dma('g', 'ldg', s_cu[:, :], c_cm[:, t0:t0 + ch], waits=ub, serial=True)
        ctx.dma('g', 'ldg', s_pv[:, :], pvt_h[u * DST:(u + 1) * DST, :], waits=ub,
                serial=True)
        ld = ctx.dtok('ld')
        ldg = ctx.dtok('ldg')

        prev = {}
        for h in range(NH0):
            for it in range(njt):
                tk = ctx.op('t', lambda te, it=it, h=h: te.matmul(
                    p_yo[it][:, :], s_cu[:, it * 128:(it + 1) * 128],
                    s_pv[:, h * HD0:(h + 1) * HD0], start=True, stop=True),
                    waits=[ldg, prev.get(('py', it))])
                ev = ctx.op('v', lambda v, it=it, h=h: v.scalar_tensor_tensor(
                    s_y[:, it * DSSM + h * HD0:it * DSSM + (h + 1) * HD0],
                    p_yo[it][:, :], s_ec[:, it * NH0 + h:it * NH0 + h + 1],
                    s_y[:, it * DSSM + h * HD0:it * DSSM + (h + 1) * HD0],
                    ALU.mult, ALU.add), waits=[tk, ld])
                prev[('py', it)] = ev
        for jt in range(njt):
            yt = s_y[:, jt * DSSM:(jt + 1) * DSSM]
            zt = s_z[:, jt * DSSM:(jt + 1) * DSSM]
            ot = s_o[:, jt * DSSM:(jt + 1) * DSSM]
            vt = ctx.tok('v')
            t1 = ctx.op('s', lambda s, zt=zt: s.activation(zt, zt, AF.Silu), waits=[vt, ld])
            t2 = ctx.op('v', lambda v, yt=yt, zt=zt: v.tensor_tensor(yt, yt, zt, ALU.mult),
                        waits=[t1, vt])
            t3 = ctx.op('s', lambda s, yt=yt: s.activation(s_sq[:, :], yt, AF.Square,
                                                           accum_out=s_ms[:, 0:1]),
                        waits=[t2])
            t4 = ctx.op('s', lambda s: s.activation(s_ms[:, 1:2], s_ms[:, 0:1], AF.Sqrt,
                                                    scale=1.0 / DSSM, bias=eps_ap[:, 0:1]),
                        waits=[t3, eps_tok])
            t5 = ctx.op('v', lambda v: v.reciprocal(s_ms[:, 2:3], s_ms[:, 1:2]), waits=[t4])
            t6 = ctx.op('v', lambda v, yt=yt, ot=ot: v.tensor_scalar(
                ot, yt, s_ms[:, 2:3], None, ALU.mult), waits=[t5, ub[3]])
            ctx.dma('y', 'oy', yn_o[t0 + jt * 128:t0 + (jt + 1) * 128, :], ot,
                    waits=[t6], serial=True)
    ctx.emit()
    return nc


import math as _math

def _pwl_table():
    x = np.linspace(-4.0, 4.0, 33).astype(np.float32)
    erf = np.array([_math.erf(float(v) / _math.sqrt(2.0)) for v in x], np.float64)
    y = (np.float32(0.5) * x * (np.float32(1.0) + erf.astype(np.float32))).astype(np.float32)
    m = ((y[1:] - y[:-1]) / np.float32(0.25)).astype(np.float32)
    m = np.concatenate([m, np.float32([1.0])])
    c = np.empty(33, np.float32)
    c[0] = m[0]
    c[1:] = m[1:] - m[:-1]
    return x, y, c

_PWLX, _PWLY, _PWLC = _pwl_table()


def emit_round256(ctx, src_ap, dst_ap, tmp_ap, waits, tmp_guard=None):
    t1 = ctx.op('v', lambda v: v.tensor_scalar(tmp_ap, src_ap, 256.0, MAGIC,
                                               ALU.mult, ALU.add),
                waits=list(waits) + [tmp_guard])
    t2 = ctx.op('v', lambda v: v.tensor_scalar(dst_ap, tmp_ap, MAGIC, 1.0 / 256.0,
                                               ALU.subtract, ALU.mult), waits=[t1])
    return t2


GELU_MODE = 'pwl'


def emit_gelu_q(ctx, h_ap, out_ap, scr, waits):
    t1 = ctx.op('v', lambda v: v.tensor_scalar(scr['t'], h_ap, 256.0, MAGIC,
                                               ALU.mult, ALU.add),
                waits=list(waits) + [scr.get('guard')])
    if GELU_MODE == 'hw':
        t2 = ctx.op('s', lambda s: s.activation(scr['u'], scr['t'], AF.Gelu,
                                                scale=1.0 / 256.0,
                                                bias=-MAGIC / 256.0), waits=[t1])
        acc_tok = t2
        acc_ap = scr['u']
    else:
        t2 = ctx.op('v', lambda v: v.tensor_scalar(scr['u'], scr['t'], MAGIC, None,
                                                   ALU.subtract), waits=[t1])
        z0 = ctx.op('v', lambda v: v.memset(scr['acc'], float(_PWLY[0])), waits=[t1])
        acc_tok = z0
        for i in range(33):
            kx = float(256.0 * _PWLX[i])
            ci = float(_PWLC[i] / 256.0)
            tr = ctx.op('v', lambda v, kx=kx: v.tensor_scalar(
                scr['r'], scr['u'], kx, 0.0, ALU.subtract, ALU.max),
                waits=[t2, acc_tok])
            acc_tok = ctx.op('v', lambda v, ci=ci: v.scalar_tensor_tensor(
                scr['acc'], scr['r'], ci, scr['acc'], ALU.mult, ALU.add),
                waits=[tr])
        acc_ap = scr['acc']
    t3 = ctx.op('v', lambda v: v.tensor_scalar(scr['t'], acc_ap, 256.0, MAGIC,
                                               ALU.mult, ALU.add), waits=[acc_tok])
    t4 = ctx.op('v', lambda v: v.tensor_scalar(out_ap, scr['t'], MAGIC, 1.0 / 256.0,
                                               ALU.subtract, ALU.mult), waits=[t3])
    return t4


def build_ffn1(Tn=T):
    FF = 4096
    def post(nc, ctx, i, o0, odp, ot, par, tok):
        if not hasattr(ctx, '_gelu_scr'):
            scr_t = nc.alloc_sbuf_tensor('g_t', [128, Tn], F32)
            scr_u = nc.alloc_sbuf_tensor('g_u', [128, Tn], F32)
            scr_a = nc.alloc_sbuf_tensor('g_a', [128, Tn], F32)
            scr_r = nc.alloc_sbuf_tensor('g_r', [128, Tn], F32)
            gout = nc.alloc_sbuf_tensor('g_o', [128, 2 * Tn], BF16)
            ctx._gelu_scr = (scr_t, scr_u, scr_a, scr_r, gout)
            ctx._gelu_guard = None
        scr_t, scr_u, scr_a, scr_r, gout = ctx._gelu_scr
        ob = gout[:, par * Tn:(par + 1) * Tn]
        stw = ctx.dtok('st%d' % par) if ('st%d' % par) in ctx.dsem else None
        scr = {'t': scr_t[:, :], 'u': scr_u[:, :], 'acc': scr_a[:, :], 'r': scr_r[:, :],
               'guard': ctx._gelu_guard}
        tk = emit_gelu_q(ctx, ot, ob, scr, waits=[tok, stw])
        ctx._gelu_guard = tk
        return ob, tk
    return build_linear(DM, FF, Tn, bias=True, out_dtype=BF16, stage_dtype=F32,
                        post_od=post)


def build_f2b(Tn=T):
    def post(nc, ctx, i, o0, odp, ot, par, tok):
        if not hasattr(ctx, '_scr'):
            ctx._scr = nc.alloc_sbuf_tensor('r_t', [128, Tn], F32)
            ctx._g = None
        r2t = ctx._sres2[0:odp, par * Tn:par * Tn + Tn]
        lr = ctx.dma('y', 'r2%d' % par, r2t, ctx._r2ap[o0:o0 + odp, :], serial=True,
                     waits=[getattr(ctx, '_r2use%d' % par, None)])
        tk = emit_round256(ctx, ot, ot, ctx._scr[:, :], waits=[tok], tmp_guard=ctx._g)
        ctx._g = tk
        tk2 = ctx.op('v', lambda v, ot=ot, r2t=r2t: v.tensor_tensor(
            ot, ot, r2t, ALU.add), waits=[tk, lr])
        setattr(ctx, '_r2use%d' % par, tk2)
        return ot, tk2
    return build_linear(2048, DM, Tn, bias=True, residual=True, residual2=True,
                        x_hbm_dtype=BF16, post_od=post)



def build_conv_silu(C_, Tn=T):
    nc = bass.Bass('TRN2', target_bir_lowering=False)
    xh = nc.dram_tensor('x_pad', [C_, Tn + 3], F32, kind='ExternalInput').ap()
    wh = nc.dram_tensor('w', [C_, 4], F32, kind='ExternalInput').ap()
    oh = nc.dram_tensor('out', [C_, Tn], F32, kind='ExternalOutput').ap()
    nct = C_ // 128
    sx = nc.alloc_sbuf_tensor('sx', [128, 2 * (Tn + 3)], F32)
    sw = nc.alloc_sbuf_tensor('sw', [128, nct * 4], F32)
    sa = nc.alloc_sbuf_tensor('sa', [128, 2 * Tn], F32)
    so = nc.alloc_sbuf_tensor('so', [128, 2 * Tn], F32)
    ctx = Ctx(nc)
    for i in range(nct):
        ctx.dma('y', 'w', sw[:, 4 * i:4 * i + 4], wh[i * 128:(i + 1) * 128, :])
    wl = ctx.dtok('w')
    prev = {}
    for i in range(nct):
        par = i % 2
        xt = sx[:, par * (Tn + 3):par * (Tn + 3) + Tn + 3]
        at = sa[:, par * Tn:(par + 1) * Tn]
        ot = so[:, par * Tn:(par + 1) * Tn]
        stw = ctx.dtok('o%d' % par) if ('o%d' % par) in ctx.dsem else None
        lt = ctx.dma('g', 'x%d' % par, xt, xh[i * 128:(i + 1) * 128, :], serial=True,
                     waits=[prev.get(('x', par))])
        t = ctx.op('v', lambda v, xt=xt, at=at, i=i: v.tensor_scalar(
            at, xt[:, 0:Tn], sw[:, 4 * i:4 * i + 1], None, ALU.mult),
            waits=[lt, wl, prev.get(('a', par))])
        for k in range(1, 4):
            t = ctx.op('v', lambda v, xt=xt, at=at, i=i, k=k: v.scalar_tensor_tensor(
                at, xt[:, k:k + Tn], sw[:, 4 * i + k:4 * i + k + 1], at,
                ALU.mult, ALU.add), waits=[t])
        prev[('x', par)] = t
        t2 = ctx.op('s', lambda s, at=at, ot=ot: s.activation(ot, at, AF.Silu),
                    waits=[t, stw])
        prev[('a', par)] = t2
        ctx.dma('y', 'o%d' % par, oh[i * 128:(i + 1) * 128, :], ot, waits=[t2],
                serial=True)
    ctx.emit()
    return nc



def build_l2norm_qk(Tn=T):
    nc = bass.Bass('TRN2', target_bir_lowering=False)
    qh = nc.dram_tensor('q_tm', [Tn, KEY_DIM], F32, kind='ExternalInput').ap()
    kh = nc.dram_tensor('k_tm', [Tn, KEY_DIM], F32, kind='ExternalInput').ap()
    qo = nc.dram_tensor('qn', [Tn, KEY_DIM], BF16, kind='ExternalOutput').ap()
    ko = nc.dram_tensor('kn', [Tn, KEY_DIM], BF16, kind='ExternalOutput').ap()
    nr = Tn // 128
    sx = nc.alloc_sbuf_tensor('sx', [128, 2 * KEY_DIM], F32)
    sq = nc.alloc_sbuf_tensor('sq', [128, KEY_DIM], BF16)
    sm = nc.alloc_sbuf_tensor('sm', [128, 3 * H1], F32)
    so = nc.alloc_sbuf_tensor('so', [128, 2 * KEY_DIM], BF16)
    ctx = Ctx(nc)
    eps_ap, ept = ctx.const_col(1e-6, 'epsn')
    prev = {}
    for idx in range(2 * nr):
        tensor_i = idx % 2
        r = idx // 2
        src_h = qh if tensor_i == 0 else kh
        dst_h = qo if tensor_i == 0 else ko
        par = idx % 2
        xt = sx[:, par * KEY_DIM:(par + 1) * KEY_DIM]
        ot = so[:, par * KEY_DIM:(par + 1) * KEY_DIM]
        stw = ctx.dtok('o%d' % par) if ('o%d' % par) in ctx.dsem else None
        lt = ctx.dma('y', 'x%d' % par, xt, src_h[r * 128:(r + 1) * 128, :], serial=True,
                     waits=[prev.get(('u', par))])
        ts_ = []
        for h in range(H1):
            hs = slice(h * KD1, (h + 1) * KD1)
            t1 = ctx.op('s', lambda s, xt=xt, h=h, hs=hs: s.activation(
                sq[:, hs], xt[:, hs], AF.Square, accum_out=sm[:, h:h + 1]),
                waits=[lt, prev.get('sq')])
            t2 = ctx.op('s', lambda s, h=h: s.activation(
                sm[:, H1 + h:H1 + h + 1], sm[:, h:h + 1], AF.Sqrt), waits=[t1])
            t3 = ctx.op('v', lambda v, h=h: v.tensor_scalar(
                sm[:, H1 + h:H1 + h + 1], sm[:, H1 + h:H1 + h + 1], eps_ap[:, 0:1], None,
                ALU.max), waits=[t2, ept])
            t4 = ctx.op('v', lambda v, h=h: v.reciprocal(
                sm[:, 2 * H1 + h:2 * H1 + h + 1], sm[:, H1 + h:H1 + h + 1]), waits=[t3])
            t5 = ctx.op('v', lambda v, xt=xt, ot=ot, h=h, hs=hs: v.tensor_scalar(
                ot[:, hs], xt[:, hs], sm[:, 2 * H1 + h:2 * H1 + h + 1], None, ALU.mult),
                waits=[t4, stw])
            ts_.append(t5)
        prev[('u', par)] = ts_[-1]
        prev['sq'] = ts_[-1]
        ctx.dma('y', 'o%d' % par, dst_h[r * 128:(r + 1) * 128, :], ot,
                waits=[ts_[-1]], serial=True)
    ctx.emit()
    return nc



def build_ab_rows(Tn=T, ch=CH1):
    nch = Tn // ch
    nc = bass.Bass('TRN2', target_bir_lowering=False)
    ar = nc.dram_tensor('a_raw', [H1, Tn], F32, kind='ExternalInput').ap()
    br = nc.dram_tensor('b_raw', [H1, Tn], F32, kind='ExternalInput').ap()
    dtb = nc.dram_tensor('dt_bias', [H1, 1], F32, kind='ExternalInput').ap()
    alog = nc.dram_tensor('A_log', [H1, 1], F32, kind='ExternalInput').ap()
    lg_o = nc.dram_tensor('lg', [H1, Tn], F32, kind='ExternalOutput').ap()
    be_o = nc.dram_tensor('beta', [H1, Tn], F32, kind='ExternalOutput').ap()
    s = nc.alloc_sbuf_tensor('s', [H1, 6 * Tn], F32)
    sc = nc.alloc_sbuf_tensor('sc', [H1, 4], F32)
    U, AB, G, LG, BE, ZERO = (s[:, i * Tn:(i + 1) * Tn] for i in range(6))
    ctx = Ctx(nc)
    one_ap, one_tok = ctx.const_col(1.0, 'one')
    ctx.dma('y', 'ld', U, ar[:, :])
    ctx.dma('y', 'ld', BE, br[:, :])
    ctx.dma('y', 'ld', sc[:, 0:1], dtb[:, :])
    ctx.dma('y', 'ld', sc[:, 1:2], alog[:, :])
    ld = ctx.dtok('ld')
    z0 = ctx.op('v', lambda v: v.memset(ZERO, 0.0))
    t = ctx.op('v', lambda v: v.tensor_scalar(U, U, sc[:, 0:1], None, ALU.add), waits=[ld])
    t = ctx.op('s', lambda s_: s_.activation(AB, U, AF.Abs), waits=[t])
    t = ctx.op('s', lambda s_: s_.activation(AB, AB, AF.Exp, scale=-1.0), waits=[t])
    t = ctx.op('s', lambda s_: s_.activation(AB, AB, AF.Ln, bias=one_ap[0:H1, 0:1]),
               waits=[t, one_tok])
    t2 = ctx.op('v', lambda v: v.tensor_scalar(G, U, 0.0, None, ALU.max), waits=[t])
    tsp = ctx.op('v', lambda v: v.tensor_tensor(G, AB, G, ALU.add), waits=[t, t2])
    te = ctx.op('s', lambda s_: s_.activation(sc[:, 2:3], sc[:, 1:2], AF.Exp), waits=[ld])
    tn = ctx.op('v', lambda v: v.tensor_scalar(sc[:, 2:3], sc[:, 2:3], -1.0, None,
                                               ALU.mult), waits=[te])
    tg = ctx.op('v', lambda v: v.tensor_scalar(G, G, sc[:, 2:3], None, ALU.mult),
                waits=[tsp, tn])
    tcs = None
    for c in range(nch):
        sl = slice(c * ch, (c + 1) * ch)
        tcs = ctx.op('v', lambda v, sl=sl: v.tensor_tensor_scan(
            LG[:, sl], G[:, sl], ZERO[:, sl], 0.0, ALU.add, ALU.add), waits=[tg, z0])
    tb = ctx.op('s', lambda s_: s_.activation(BE, BE, AF.Sigmoid), waits=[ld])
    tb = ctx.op('v', lambda v: v.tensor_scalar(BE, BE, 2.0, None, ALU.mult), waits=[tb])
    ctx.dma('y', 'o', lg_o[:, :], LG, waits=[tcs])
    ctx.dma('y', 'o', be_o[:, :], BE, waits=[tb])
    ctx.emit()
    return nc



def build_delta_local(Tn=T, ch=CH1):
    nch = Tn // ch
    nc = bass.Bass('TRN2', target_bir_lowering=False)
    qcm_h = nc.dram_tensor('q_cm', [KEY_DIM, Tn], BF16, kind='ExternalInput').ap()
    kcm_h = nc.dram_tensor('k_cm', [KEY_DIM, Tn], BF16, kind='ExternalInput').ap()
    qtm_h = nc.dram_tensor('q_tm', [Tn, KEY_DIM], BF16, kind='ExternalInput').ap()
    ktm_h = nc.dram_tensor('k_tm', [Tn, KEY_DIM], BF16, kind='ExternalInput').ap()
    vtm_h = nc.dram_tensor('v_tm', [Tn, VAL_DIM], F32, kind='ExternalInput').ap()
    lgr_h = nc.dram_tensor('lg_row', [H1, Tn], F32, kind='ExternalInput').ap()
    lgt_h = nc.dram_tensor('lg_tm', [Tn, H1], F32, kind='ExternalInput').ap()
    bet_h = nc.dram_tensor('beta_tm', [Tn, H1], F32, kind='ExternalInput').ap()
    idf_h = nc.dram_tensor('identF', [128, 128], F32, kind='ExternalInput').ap()
    id2_h = nc.dram_tensor('ident2', [ch, ch], F32, kind='ExternalInput').ap()
    mi_h = nc.dram_tensor('mask_incl', [ch, ch], F32, kind='ExternalInput').ap()
    ms_h = nc.dram_tensor('mask_strict', [ch, ch], F32, kind='ExternalInput').ap()
    on_h = nc.dram_tensor('ones1', [1, 128], F32, kind='ExternalInput').ap()
    olo_o = nc.dram_tensor('O_loc', [Tn, VAL_DIM], F32, kind='ExternalOutput').ap()
    mt_o = nc.dram_tensor('Mt', [nch * H1 * KD1, KD1], F32, kind='ExternalOutput').ap()
    sl_o = nc.dram_tensor('SlocT', [nch * H1 * KD1, VD1], F32, kind='ExternalOutput').ap()
    gt_o = nc.dram_tensor('GT', [nch * H1 * KD1, ch], F32, kind='ExternalOutput').ap()
    gc_o = nc.dram_tensor('gamc', [nch, H1], F32, kind='ExternalOutput').ap()

    sq_cm = nc.alloc_sbuf_tensor('sq_cm', [128, H1 * ch], BF16)
    sk_cm = nc.alloc_sbuf_tensor('sk_cm', [128, H1 * ch], BF16)
    sq_tm = nc.alloc_sbuf_tensor('sq_tm', [ch, KEY_DIM], F32)
    sk_tm = nc.alloc_sbuf_tensor('sk_tm', [ch, KEY_DIM], F32)
    sv_tm = nc.alloc_sbuf_tensor('sv_tm', [ch, VAL_DIM], F32)
    s_lgr = nc.alloc_sbuf_tensor('s_lgr', [1, H1 * ch], F32)
    s_lgc = nc.alloc_sbuf_tensor('s_lgc', [ch, H1], F32)
    s_bet = nc.alloc_sbuf_tensor('s_bet', [ch, H1], F32)
    s_id = nc.alloc_sbuf_tensor('s_id', [128, 128], F32)
    s_id2 = nc.alloc_sbuf_tensor('s_id2', [ch, ch], F32)
    s_mi = nc.alloc_sbuf_tensor('s_mi', [ch, ch], F32)
    s_ms = nc.alloc_sbuf_tensor('s_ms', [ch, ch], F32)
    s_on = nc.alloc_sbuf_tensor('s_on', [1, 128], F32)
    s_lgb = nc.alloc_sbuf_tensor('s_lgb', [ch, ch], F32)
    s_E = nc.alloc_sbuf_tensor('s_E', [ch, ch], F32)
    s_C = nc.alloc_sbuf_tensor('s_C', [ch, ch], F32)
    s_A = nc.alloc_sbuf_tensor('s_A', [ch, ch], F32)
    s_Dn = nc.alloc_sbuf_tensor('s_Dn', [ch, ch], F32)
    s_DT = nc.alloc_sbuf_tensor('s_DT', [ch, ch], F32)
    s_X = nc.alloc_sbuf_tensor('s_X', [ch, ch], F32)
    s_XT = nc.alloc_sbuf_tensor('s_XT', [ch, ch], F32)
    s_Z = nc.alloc_sbuf_tensor('s_Z', [ch, ch], F32)
    s_Rv = nc.alloc_sbuf_tensor('s_Rv', [ch, VD1], F32)
    s_Kp = nc.alloc_sbuf_tensor('s_Kp', [ch, KD1], F32)
    s_Wl = nc.alloc_sbuf_tensor('s_Wl', [ch, VD1], F32)
    s_Pl = nc.alloc_sbuf_tensor('s_Pl', [ch, KD1], F32)
    s_Qt = nc.alloc_sbuf_tensor('s_Qt', [ch, KD1], F32)
    s_G = nc.alloc_sbuf_tensor('s_G', [ch, KD1], F32)
    s_GT = nc.alloc_sbuf_tensor('s_GT', [KD1, ch], F32)
    s_LK = nc.alloc_sbuf_tensor('s_LK', [ch, KD1], F32)
    s_Sl = nc.alloc_sbuf_tensor('s_Sl', [KD1, VD1], F32)
    s_Mt = nc.alloc_sbuf_tensor('s_Mt', [KD1, KD1], F32)
    s_col = nc.alloc_sbuf_tensor('s_col', [ch, 4], F32)
    s_gr = nc.alloc_sbuf_tensor('s_gr', [1, ch], F32)
    s_gc = nc.alloc_sbuf_tensor('s_gc', [1, H1], F32)
    s_O = nc.alloc_sbuf_tensor('s_O', [ch, VAL_DIM], F32)

    p1 = nc.alloc_psum_tensor('p1', [ch, ch], F32)
    p2 = nc.alloc_psum_tensor('p2', [ch, ch], F32)
    pP = nc.alloc_psum_tensor('pP', [ch, KD1], F32)
    pT = nc.alloc_psum_tensor('pT', [KD1, ch], F32)
    pW = nc.alloc_psum_tensor('pW', [ch, VD1], F32)
    pS = nc.alloc_psum_tensor('pS', [KD1, VD1], F32)
    pM = nc.alloc_psum_tensor('pM', [KD1, KD1], F32)

    ctx = Ctx(nc)
    for src_h, dst in [(idf_h, s_id), (id2_h, s_id2), (mi_h, s_mi), (ms_h, s_ms),
                       (on_h, s_on)]:
        ctx.dma('y', 'const', dst[:, :], src_h[:, :])
    ctok = ctx.dtok('const')

    for u in range(nch):
        t0 = u * ch
        ub = [ctx.tok('v'), ctx.tok('t'), ctx.tok('s')] +              [ctx.dtok(n) for n in ('oO', 'oM', 'oS', 'oG', 'ogc') if n in ctx.dsem]
        ctx.dma('g', 'ld', sq_cm[:, :].rearrange('p (a w) -> p a w', w=ch),
                qcm_h[:, t0:t0 + ch].rearrange('(a p) w -> p a w', p=128),
                waits=ub, serial=True)
        ctx.dma('g', 'ld', sk_cm[:, :].rearrange('p (a w) -> p a w', w=ch),
                kcm_h[:, t0:t0 + ch].rearrange('(a p) w -> p a w', p=128),
                waits=ub, serial=True)
        ctx.dma('g', 'ld', sq_tm[:, :], qtm_h[t0:t0 + ch, :], waits=ub, serial=True)
        ctx.dma('g', 'ld', sk_tm[:, :], ktm_h[t0:t0 + ch, :], waits=ub, serial=True)
        ctx.dma('y', 'ldh', sv_tm[:, :], vtm_h[t0:t0 + ch, :], waits=ub, serial=True)
        ctx.dma('y', 'ldh', s_lgr[:, :].rearrange('o (a w) -> o a w', w=ch),
                lgr_h[:, t0:t0 + ch].unsqueeze(0), waits=ub, serial=True)
        ctx.dma('y', 'ldh', s_lgc[:, :], lgt_h[t0:t0 + ch, :], waits=ub, serial=True)
        ctx.dma('y', 'ldh', s_bet[:, :], bet_h[t0:t0 + ch, :], waits=ub, serial=True)
        ld = ctx.dtok('ld')
        ldh = ctx.dtok('ldh')

        def emit_head(h, u=u, ld=ld, ldh=ldh, ub=ub):
            hs = slice(h * ch, (h + 1) * ch)
            kcm = sk_cm[:, hs]
            qcm = sq_cm[:, hs]
            ktm = sk_tm[:, h * KD1:(h + 1) * KD1]
            qtm = sq_tm[:, h * KD1:(h + 1) * KD1]
            vtm = sv_tm[:, h * VD1:(h + 1) * VD1]
            lgc = s_lgc[:, h:h + 1]
            bet = s_bet[:, h:h + 1]

            def op(e, f, w=()):
                return ctx.op(e, f, waits=[x for x in w if x is not None])

            m1 = op('t', lambda te: te.matmul(p1[:, :], s_on[0:1, 0:ch],
                                              s_lgr[0:1, hs], start=True, stop=True),
                    [ldh, ctok, ctx._pv.get('p1') if hasattr(ctx, '_pv') else None])
            e1 = op('v', lambda v: v.tensor_copy(s_lgb[:, :], p1[:, :]),
                    [m1, ctx._pv.get('lgb') if hasattr(ctx, '_pv') else None,
                     ctx._pv.get('lgb_s') if hasattr(ctx, '_pv') else None])
            g1 = op('s', lambda s_: s_.activation(s_gr[:, :], s_lgb[0:1, :], AF.Exp), [e1])
            g2 = op('v', lambda v, h=h: v.tensor_copy(s_gc[:, h:h + 1], s_gr[:, ch - 1:ch]),
                    [g1])
            c1 = op('s', lambda s_: s_.activation(s_col[:, 0:1], lgc, AF.Exp), [ldh])
            c2 = op('s', lambda s_: s_.activation(s_col[:, 1:2], lgc, AF.Exp, scale=-1.0,
                                                  bias=s_lgb[:, ch - 1:ch]), [ldh, e1])
            c3 = op('v', lambda v: v.tensor_tensor(s_col[:, 2:3], s_col[:, 0:1], bet,
                                                   ALU.mult), [c1, ldh])
            e2 = op('v', lambda v: v.tensor_scalar(s_E[:, :], s_lgb[:, :], lgc, 0.0,
                                                   ALU.subtract, ALU.min), [e1, ldh])
            e3 = op('s', lambda s_: s_.activation(s_E[:, :], s_E[:, :], AF.Exp), [e2])
            m2 = op('t', lambda te: te.matmul(p2[:, :], kcm, kcm, start=True, stop=True),
                    [ld, ctx._pv.get('p2') if hasattr(ctx, '_pv') else None])
            e4 = op('v', lambda v: v.scalar_tensor_tensor(s_C[:, :], p2[:, :], 0.0,
                                                          s_ms[:, :], ALU.add, ALU.mult),
                    [m2, ctok])
            e5 = op('v', lambda v: v.tensor_tensor(s_C[:, :], s_C[:, :], s_E[:, :],
                                                   ALU.mult), [e4, e3])
            m3 = op('t', lambda te: te.matmul(p2[:, :], kcm, qcm, start=True, stop=True),
                    [ld, e4])
            e6 = op('v', lambda v: v.scalar_tensor_tensor(s_A[:, :], p2[:, :], 0.0,
                                                          s_mi[:, :], ALU.add, ALU.mult),
                    [m3, ctok])
            e7 = op('v', lambda v: v.tensor_tensor(s_A[:, :], s_A[:, :], s_E[:, :],
                                                   ALU.mult), [e6, e3])
            m4 = op('t', lambda te: te.transpose(p1[:, :], s_C[:, :], s_id[0:ch, 0:ch]),
                    [e5, e1, ctok])
            e8 = op('v', lambda v: v.scalar_tensor_tensor(s_Dn[:, :], p1[:, :], bet,
                                                          s_id[0:ch, 0:ch], ALU.mult,
                                                          ALU.add), [m4, ctok])
            m5 = op('t', lambda te: te.transpose(p1[:, :], s_Dn[:, :], s_id[0:ch, 0:ch]),
                    [e8])
            e9 = op('v', lambda v: v.tensor_copy(s_DT[:, :], p1[:, :]), [m5])
            x0 = op('v', lambda v: v.scalar_tensor_tensor(s_X[:, :], s_Dn[:, :], -1.0,
                                                          s_id2[:, :], ALU.mult, ALU.add),
                    [e8])
            xt_tok = None
            for itn in range(5):
                mA = op('t', lambda te: te.matmul(p2[:, :], s_DT[:, :], s_X[:, :],
                                                  start=True, stop=True),
                        [e9, x0 if itn == 0 else xt_tok, e6])
                zA = op('v', lambda v: v.scalar_tensor_tensor(s_Z[:, :], p2[:, :], -1.0,
                                                              s_id2[:, :], ALU.mult,
                                                              ALU.add), [mA])
                mB = op('t', lambda te: te.transpose(p1[:, :], s_X[:, :],
                                                     s_id[0:ch, 0:ch]), [zA])
                zB = op('v', lambda v: v.tensor_copy(s_XT[:, :], p1[:, :]), [mB])
                mC = op('t', lambda te: te.matmul(p2[:, :], s_XT[:, :], s_Z[:, :],
                                                  start=True, stop=True), [zB, zA])
                xt_tok = op('v', lambda v: v.tensor_copy(s_X[:, :], p2[:, :]), [mC])
            mD = op('t', lambda te: te.transpose(p1[:, :], s_X[:, :], s_id[0:ch, 0:ch]),
                    [xt_tok])
            xT = op('v', lambda v: v.tensor_copy(s_XT[:, :], p1[:, :]), [mD])
            r1 = op('v', lambda v: v.tensor_scalar(s_Rv[:, :], vtm, bet, None, ALU.mult),
                    [ldh])
            r2 = op('v', lambda v: v.tensor_scalar(s_Kp[:, :], ktm, s_col[:, 2:3], None,
                                                   ALU.mult), [ld, c3])
            mW = op('t', lambda te: te.matmul(pW[:, :], s_XT[:, :], s_Rv[:, :],
                                              start=True, stop=True), [xT, r1])
            eW = op('v', lambda v: v.tensor_copy(s_Wl[:, :], pW[:, :]), [mW])
            mP = op('t', lambda te: te.matmul(pP[:, :], s_XT[:, :], s_Kp[:, :],
                                              start=True, stop=True), [xT, r2])
            eP = op('v', lambda v: v.tensor_copy(s_Pl[:, :], pP[:, :]), [mP])
            mO = op('t', lambda te: te.matmul(pW[:, :], s_A[:, :], s_Wl[:, :],
                                              start=True, stop=True), [e7, eW])
            eO = op('v', lambda v, h=h: v.tensor_copy(s_O[:, h * VD1:(h + 1) * VD1],
                                                      pW[:, :]), [mO] + ub)
            q1 = op('v', lambda v: v.tensor_scalar(s_Qt[:, :], qtm, s_col[:, 0:1], None,
                                                   ALU.mult), [ld, c1])
            mG = op('t', lambda te: te.matmul(pP[:, :], s_A[:, :], s_Pl[:, :],
                                              start=True, stop=True), [e7, eP, eO])
            eG = op('v', lambda v: v.scalar_tensor_tensor(s_G[:, :], pP[:, :], -1.0,
                                                          s_Qt[:, :], ALU.mult, ALU.add),
                    [mG, q1])
            mGT = op('t', lambda te: te.transpose(pT[:, :], s_G[:, :], s_id[0:ch, 0:ch]),
                     [eG])
            eGT = op('v', lambda v: v.tensor_copy(s_GT[:, :], pT[:, :]),
                     [mGT, ctx.dtok('oG') if 'oG' in ctx.dsem else None])
            l1 = op('v', lambda v: v.tensor_scalar(s_LK[:, :], ktm, s_col[:, 1:2], None,
                                                   ALU.mult), [ld, c2])
            mS = op('t', lambda te: te.matmul(pS[:, :], s_LK[:, :], s_Wl[:, :],
                                              start=True, stop=True), [l1, eW])
            eS = op('v', lambda v: v.tensor_copy(s_Sl[:, :], pS[:, :]),
                    [mS, ctx.dtok('oS') if 'oS' in ctx.dsem else None])
            mM = op('t', lambda te: te.matmul(pM[:, :], s_Pl[:, :], s_LK[:, :],
                                              start=True, stop=True), [eP, l1])
            eM = op('v', lambda v: v.tensor_copy(s_Mt[:, :], pM[:, :]),
                    [mM, ctx.dtok('oM') if 'oM' in ctx.dsem else None])
            r0 = (u * H1 + h) * KD1
            ctx.dma('y', 'oS', sl_o[r0:r0 + KD1, :], s_Sl[:, :], waits=[eS], serial=True)
            ctx.dma('y', 'oM', mt_o[r0:r0 + KD1, :], s_Mt[:, :], waits=[eM], serial=True)
            ctx.dma('y', 'oG', gt_o[r0:r0 + KD1, :], s_GT[:, :], waits=[eGT], serial=True)
            if not hasattr(ctx, '_pv'):
                ctx._pv = {}
            ctx._pv['p1'] = xT
            ctx._pv['p2'] = xt_tok
            ctx._pv['lgb'] = ctx.tok('v')
            ctx._pv['lgb_s'] = ctx.tok('s')

        for h in range(H1):
            emit_head(h)
        vt = ctx.tok('v')
        ctx.dma('y', 'oO', olo_o[t0:t0 + ch, :], s_O[:, :], waits=[vt], serial=True)
        ctx.dma('y', 'ogc', gc_o[u:u + 1, :], s_gc[:, :], waits=[vt], serial=True)

    ctx.emit()
    return nc



def build_delta_chain(nunits, nchg, ch=CH1):
    nc = bass.Bass('TRN2', target_bir_lowering=False)
    mt_h = nc.dram_tensor('Mt', [nunits * nchg * KD1, KD1], F32, kind='ExternalInput').ap()
    sl_h = nc.dram_tensor('SlocT', [nunits * nchg * KD1, VD1], F32, kind='ExternalInput').ap()
    gt_h = nc.dram_tensor('GT', [nunits * nchg * KD1, ch], F32, kind='ExternalInput').ap()
    gm_h = nc.dram_tensor('gam_bc', [nunits * nchg, 128], F32, kind='ExternalInput').ap()
    oc_o = nc.dram_tensor('O_corr', [nunits * nchg * ch, VD1], F32, kind='ExternalOutput').ap()
    s_S0 = nc.alloc_sbuf_tensor('s_S0', [128, VD1], F32)
    s_tmp = nc.alloc_sbuf_tensor('s_tmp', [128, VD1], F32)
    s_mt = nc.alloc_sbuf_tensor('s_mt', [128, 2 * KD1], F32)
    s_sl = nc.alloc_sbuf_tensor('s_sl', [128, 2 * VD1], F32)
    s_gt = nc.alloc_sbuf_tensor('s_gt', [128, 2 * ch], F32)
    s_gm = nc.alloc_sbuf_tensor('s_gm', [128, 2], F32)
    s_oc = nc.alloc_sbuf_tensor('s_oc', [ch, 2 * VD1], F32)
    pC = nc.alloc_psum_tensor('pC', [128, VD1], F32)
    pO = nc.alloc_psum_tensor('pO', [ch, VD1], F32)
    ctx = Ctx(nc)
    for un in range(nunits):
        z = ctx.op('v', lambda v: v.memset(s_S0[:, :], 0.0),
                   waits=[ctx._last] if hasattr(ctx, '_last') else [])
        upd = z
        for c in range(nchg):
            par = c % 2
            r0 = (un * nchg + c) * KD1
            lw = [ctx.tok('v')] if c >= 2 or un > 0 else []
            l1 = ctx.dma('g', 'ld%d' % par, s_mt[:, par * KD1:(par + 1) * KD1],
                         mt_h[r0:r0 + KD1, :], waits=lw, serial=True)
            l2 = ctx.dma('g', 'ld%d' % par, s_sl[:, par * VD1:(par + 1) * VD1],
                         sl_h[r0:r0 + KD1, :], serial=True)
            l3 = ctx.dma('g', 'ld%d' % par, s_gt[:, par * ch:(par + 1) * ch],
                         gt_h[r0:r0 + KD1, :], serial=True)
            l4 = ctx.dma('y', 'lh%d' % par, s_gm[:, par:par + 1],
                         gm_h[un * nchg + c:un * nchg + c + 1, :].transpose([1, 0]),
                         serial=True)
            mO = ctx.op('t', lambda te, par=par: te.matmul(
                pO[:, :], s_gt[:, par * ch:(par + 1) * ch], s_S0[:, :],
                start=True, stop=True),
                waits=[l3, upd, ctx._po if hasattr(ctx, '_po') else None])
            eO = ctx.op('v', lambda v, par=par: v.tensor_copy(
                s_oc[:, par * VD1:(par + 1) * VD1], pO[:, :]),
                waits=[mO, ctx.dtok('oc%d' % par) if ('oc%d' % par) in ctx.dsem else None])
            ctx._po = eO
            ro = (un * nchg + c) * ch
            ctx.dma('y', 'oc%d' % par, oc_o[ro:ro + ch, :],
                    s_oc[:, par * VD1:(par + 1) * VD1], waits=[eO], serial=True)
            mC = ctx.op('t', lambda te, par=par: te.matmul(
                pC[:, :], s_mt[:, par * KD1:(par + 1) * KD1], s_S0[:, :],
                start=True, stop=True),
                waits=[l1, upd, ctx._pc if hasattr(ctx, '_pc') else None])
            t1 = ctx.op('v', lambda v, par=par: v.scalar_tensor_tensor(
                s_tmp[:, :], s_S0[:, :], s_gm[:, par:par + 1], pC[:, :],
                ALU.mult, ALU.subtract), waits=[mC, l4, mO])
            ctx._pc = t1
            upd = ctx.op('v', lambda v, par=par: v.tensor_tensor(
                s_S0[:, :], s_tmp[:, :], s_sl[:, par * VD1:(par + 1) * VD1], ALU.add),
                waits=[t1, l2])
        ctx._last = upd
    ctx.emit()
    return nc



def build_gate_out(Tn=T):
    nc = bass.Bass('TRN2', target_bir_lowering=False)
    ol_h = nc.dram_tensor('O_loc', [Tn, VAL_DIM], F32, kind='ExternalInput').ap()
    oc_h = nc.dram_tensor('O_corr', [Tn, VAL_DIM], F32, kind='ExternalInput').ap()
    g_h = nc.dram_tensor('g_tm', [Tn, VAL_DIM], F32, kind='ExternalInput').ap()
    og_o = nc.dram_tensor('og', [Tn, VAL_DIM], BF16, kind='ExternalOutput').ap()
    nr = Tn // 128
    sA = nc.alloc_sbuf_tensor('sA', [128, 2 * VAL_DIM], F32)
    sB = nc.alloc_sbuf_tensor('sB', [128, VAL_DIM], F32)
    sG = nc.alloc_sbuf_tensor('sG', [128, VAL_DIM], F32)
    sq = nc.alloc_sbuf_tensor('sq', [128, VAL_DIM], BF16)
    sm = nc.alloc_sbuf_tensor('sm', [128, 3 * H1], F32)
    so = nc.alloc_sbuf_tensor('so', [128, 2 * VAL_DIM], BF16)
    ctx = Ctx(nc)
    eps_ap, ept = ctx.const_col(1e-5, 'epsr')
    prev = {}
    for r in range(nr):
        par = r % 2
        at = sA[:, par * VAL_DIM:(par + 1) * VAL_DIM]
        ot = so[:, par * VAL_DIM:(par + 1) * VAL_DIM]
        l1 = ctx.dma('y', 'lA%d' % par, at, ol_h[r * 128:(r + 1) * 128, :],
                     waits=[prev.get(('a', par))], serial=True)
        l2 = ctx.dma('y', 'lB', sB[:, :], oc_h[r * 128:(r + 1) * 128, :],
                     waits=[prev.get('b')], serial=True)
        l3 = ctx.dma('y', 'lG', sG[:, :], g_h[r * 128:(r + 1) * 128, :],
                     waits=[prev.get('g2')], serial=True)
        t1 = ctx.op('v', lambda v, at=at: v.tensor_tensor(at, at, sB[:, :], ALU.add),
                    waits=[l1, l2])
        prev['b'] = t1
        t2 = ctx.op('s', lambda s_: s_.activation(sG[:, :], sG[:, :], AF.Silu),
                    waits=[l3, prev.get('g2')])
        prev['g'] = t2
        prev['g2'] = None
        hts = []
        for h in range(H1):
            hs = slice(h * VD1, (h + 1) * VD1)
            q1 = ctx.op('s', lambda s_, at=at, hs=hs, h=h: s_.activation(
                sq[:, hs], at[:, hs], AF.Square, accum_out=sm[:, h:h + 1]),
                waits=[t1, prev.get('sq')])
            q2 = ctx.op('s', lambda s_, h=h: s_.activation(
                sm[:, H1 + h:H1 + h + 1], sm[:, h:h + 1], AF.Sqrt, scale=1.0 / VD1,
                bias=eps_ap[:, 0:1]), waits=[q1, ept])
            q3 = ctx.op('v', lambda v, h=h: v.reciprocal(
                sm[:, 2 * H1 + h:2 * H1 + h + 1], sm[:, H1 + h:H1 + h + 1]), waits=[q2])
            stw = ctx.dtok('o%d' % par) if ('o%d' % par) in ctx.dsem else None
            q4 = ctx.op('v', lambda v, at=at, ot=ot, hs=hs, h=h: v.tensor_scalar(
                at[:, hs], at[:, hs], sm[:, 2 * H1 + h:2 * H1 + h + 1], None, ALU.mult),
                waits=[q3])
            q5 = ctx.op('v', lambda v, at=at, ot=ot, hs=hs: v.tensor_tensor(
                ot[:, hs], at[:, hs], sG[:, hs], ALU.mult), waits=[q4, t2, stw])
            hts.append(q5)
        prev['sq'] = hts[-1]
        prev[('a', par)] = hts[-1]
        prev['g2'] = hts[-1]
        ctx.dma('y', 'o%d' % par, og_o[r * 128:(r + 1) * 128, :], ot, waits=[hts[-1]],
                serial=True)
    ctx.emit()
    return nc



def build_amax_part(C_, Tn=T):
    nc = bass.Bass('TRN2', target_bir_lowering=False)
    x_h = nc.dram_tensor('x', [C_, Tn], F32, kind='ExternalInput').ap()
    idf_h = nc.dram_tensor('identF', [128, 128], F32, kind='ExternalInput').ap()
    p_o = nc.dram_tensor('part', [1, 1], F32, kind='ExternalOutput').ap()
    nk = C_ // 128
    sx = nc.alloc_sbuf_tensor('sx', [128, 2 * Tn], F32)
    sc = nc.alloc_sbuf_tensor('sc', [128, nk + 2], F32)
    sr = nc.alloc_sbuf_tensor('sr', [1, 129], F32)
    s_id = nc.alloc_sbuf_tensor('s_id', [128, 128], F32)
    pt = nc.alloc_psum_tensor('pt', [1, 128], F32)
    ctx = Ctx(nc)
    ctx.dma('y', 'c', s_id[:, :], idf_h[:, :])
    ctok = ctx.dtok('c')
    prev = {}
    toks = []
    for k in range(nk):
        par = k % 2
        xt = sx[:, par * Tn:(par + 1) * Tn]
        lt = ctx.dma('g', 'x%d' % par, xt, x_h[k * 128:(k + 1) * 128, :],
                     waits=[prev.get(par)], serial=True)
        tk = ctx.op('v', lambda v, xt=xt, k=k: v.reduce_max(
            sc[:, k:k + 1], xt, axis=AXL.X, apply_absolute_value=True), waits=[lt])
        prev[par] = tk
        toks.append(tk)
    t1 = ctx.op('v', lambda v: v.reduce_max(sc[:, nk:nk + 1], sc[:, 0:nk], axis=AXL.X),
                waits=[toks[-1]])
    m1 = ctx.op('t', lambda te: te.transpose(pt[:, :], sc[:, nk:nk + 1], s_id[:, :]),
                waits=[t1, ctok])
    e1 = ctx.op('v', lambda v: v.tensor_copy(sr[:, 0:128], pt[:, :]), waits=[m1])
    t2 = ctx.op('v', lambda v: v.reduce_max(sr[:, 128:129], sr[:, 0:128], axis=AXL.X),
                waits=[e1])
    ctx.dma('y', 'o', p_o[:, :], sr[:, 128:129], waits=[t2])
    ctx.emit()
    return nc



def build_ffn_fp8(Tn=T):
    nc = bass.Bass('TRN2', target_bir_lowering=False)
    parts_h = nc.dram_tensor('parts', [1, NCORES], F32, kind='ExternalInput').ap()
    x_h = nc.dram_tensor('x', [DM, Tn], F32, kind='ExternalInput').ap()
    w_h = nc.dram_tensor('w', [DM, DM], F32, kind='ExternalInput').ap()
    b_h = nc.dram_tensor('bias', [DM, 1], F32, kind='ExternalInput').ap()
    r_h = nc.dram_tensor('resid', [DM, Tn], F32, kind='ExternalInput').ap()
    on_h = nc.dram_tensor('ones1', [1, 128], F32, kind='ExternalInput').ap()
    idf_h = nc.dram_tensor('identF', [128, 128], F32, kind='ExternalInput').ap()
    out_o = nc.dram_tensor('out', [DM, Tn], F32, kind='ExternalOutput').ap()
    nk = DM // 128
    sxf = nc.alloc_sbuf_tensor('sxf', [128, 2 * Tn], F32)
    sx8 = nc.alloc_sbuf_tensor('sx8', [128, Tn], F8)
    sxq = nc.alloc_sbuf_tensor('sxq', [128, nk * Tn], BF16)
    swf = nc.alloc_sbuf_tensor('swf', [128, nk * DM], F32)
    sw8 = nc.alloc_sbuf_tensor('sw8', [128, DM], F8)
    swq = nc.alloc_sbuf_tensor('swq', [128, nk * DM], BF16)
    sbias = nc.alloc_sbuf_tensor('sbias', [128, nk], F32)
    sres = nc.alloc_sbuf_tensor('sres', [128, nk * Tn], F32)
    s_sc = nc.alloc_sbuf_tensor('s_sc', [1, 32], F32)
    s_cx = nc.alloc_sbuf_tensor('s_cx', [128, 3], F32)
    s_id = nc.alloc_sbuf_tensor('s_id', [128, 128], F32)
    s_on = nc.alloc_sbuf_tensor('s_on', [1, 128], F32)
    s_mx = nc.alloc_sbuf_tensor('s_mx', [128, nk + 2], F32)
    sr = nc.alloc_sbuf_tensor('sr', [1, 129], F32)
    sout = nc.alloc_sbuf_tensor('sout', [128, 2 * Tn], F32)
    pz = [nc.alloc_psum_tensor('pz%d' % i, [128, 512], F32) for i in range(2)]
    pt = nc.alloc_psum_tensor('pt', [1, 128], F32)
    pc = nc.alloc_psum_tensor('pc', [128, 3], F32)
    ctx = Ctx(nc)
    ctx.dma('y', 'c', s_id[:, :], idf_h[:, :])
    ctx.dma('y', 'c', s_on[:, :], on_h[:, :])
    ctx.dma('y', 'c', s_sc[:, 0:NCORES], parts_h[:, :])
    for k in range(nk):
        ctx.dma('y', 'c', swf[:, k * DM:(k + 1) * DM], w_h[k * 128:(k + 1) * 128, :])
        ctx.dma('y', 'c', sbias[:, k:k + 1], b_h[k * 128:(k + 1) * 128, :])
        ctx.dma('y', 'c', sres[:, k * Tn:(k + 1) * Tn], r_h[k * 128:(k + 1) * 128, :])
    ctok = ctx.dtok('c')
    t1 = ctx.op('v', lambda v: v.reduce_max(s_sc[:, 8:9], s_sc[:, 0:NCORES], axis=AXL.X),
                waits=[ctok])
    wts = []
    for k in range(nk):
        wts.append(ctx.op('v', lambda v, k=k: v.reduce_max(
            s_mx[:, k:k + 1], swf[:, k * DM:(k + 1) * DM], axis=AXL.X,
            apply_absolute_value=True), waits=[ctok]))
    t2 = ctx.op('v', lambda v: v.reduce_max(s_mx[:, nk:nk + 1], s_mx[:, 0:nk],
                                            axis=AXL.X), waits=[wts[-1]])
    m1 = ctx.op('t', lambda te: te.transpose(pt[:, :], s_mx[:, nk:nk + 1], s_id[:, :]),
                waits=[t2, ctok])
    e1 = ctx.op('v', lambda v: v.tensor_copy(sr[:, 0:128], pt[:, :]), waits=[m1])
    t3 = ctx.op('v', lambda v: v.reduce_max(s_sc[:, 9:10], sr[:, 0:128], axis=AXL.X),
                waits=[e1])
    t4 = ctx.op('v', lambda v: v.tensor_scalar(s_sc[:, 10:12], s_sc[:, 8:10], 1e-12, None,
                                               ALU.max), waits=[t3, t1])
    t5 = ctx.op('v', lambda v: v.reciprocal(s_sc[:, 12:14], s_sc[:, 10:12]), waits=[t4])
    t6 = ctx.op('v', lambda v: v.tensor_scalar(s_sc[:, 14:16], s_sc[:, 12:14], 224.0,
                                               None, ALU.mult), waits=[t5])
    t7 = ctx.op('v', lambda v: v.tensor_tensor(s_sc[:, 16:17], s_sc[:, 14:15],
                                               s_sc[:, 15:16], ALU.mult), waits=[t6])
    t8 = ctx.op('v', lambda v: v.reciprocal(s_sc[:, 17:18], s_sc[:, 16:17]), waits=[t7])
    m2 = ctx.op('t', lambda te: te.matmul(pc[:, 0:3], s_on[0:1, :],
                                          s_sc[0:1, 14:17].copy() if False else
                                          s_sc[0:1, 14:17], start=True, stop=True),
                waits=[t8, ctok])
    e2 = ctx.op('v', lambda v: v.tensor_copy(s_cx[:, 0:3], pc[:, 0:3]), waits=[m2])
    e2b = ctx.op('v', lambda v: v.reciprocal(s_cx[:, 2:3], s_cx[:, 2:3]), waits=[e2])
    wq_toks = []
    for k in range(nk):
        a = ctx.op('v', lambda v, k=k: v.tensor_scalar(
            swf[:, k * DM:(k + 1) * DM], swf[:, k * DM:(k + 1) * DM], s_cx[:, 1:2], None,
            ALU.mult), waits=[e2, t2 if k == 0 else None])
        b_ = ctx.op('v', lambda v, k=k: v.tensor_copy(sw8[:, 0:DM],
                                                      swf[:, k * DM:(k + 1) * DM]),
                    waits=[a, wq_toks[-1] if wq_toks else None])
        c_ = ctx.op('v', lambda v, k=k: v.tensor_copy(swq[:, k * DM:(k + 1) * DM],
                                                      sw8[:, 0:DM]), waits=[b_])
        wq_toks.append(c_)
    xq_toks = []
    prevx = {}
    for k in range(nk):
        par = k % 2
        xt = sxf[:, par * Tn:(par + 1) * Tn]
        lt = ctx.dma('g', 'x%d' % par, xt, x_h[k * 128:(k + 1) * 128, :],
                     waits=[prevx.get(par)], serial=True)
        a = ctx.op('v', lambda v, xt=xt: v.tensor_scalar(xt, xt, s_cx[:, 0:1], None,
                                                         ALU.mult), waits=[lt, e2])
        b_ = ctx.op('v', lambda v, xt=xt: v.tensor_copy(sx8[:, :], xt),
                    waits=[a, xq_toks[-1] if xq_toks else None])
        prevx[par] = b_
        c_ = ctx.op('v', lambda v, k=k: v.tensor_copy(sxq[:, k * Tn:(k + 1) * Tn],
                                                      sx8[:, :]), waits=[b_])
        xq_toks.append(c_)
    xq = ctx.tok('v')
    prev_ev = {}
    for i in range(nk):
        par = i % 2
        ot = sout[:, par * Tn:(par + 1) * Tn]
        evs = []
        tbk = min(512, Tn)
        for j in range(cdiv(Tn, tbk)):
            t0j = j * tbk
            tn = min(tbk, Tn - t0j)
            bank = (i * cdiv(Tn, tbk) + j) % 2
            ps = pz[bank]
            for k in range(nk):
                def mm(te, k=k, i=i, t0j=t0j, tn=tn, ps=ps):
                    return te.matmul(ps[:, 0:tn],
                                     swq[:, k * DM + i * 128:k * DM + (i + 1) * 128],
                                     sxq[:, k * Tn + t0j:k * Tn + t0j + tn],
                                     start=(k == 0), stop=(k == nk - 1))
                w8 = [xq, prev_ev.get(bank)] if k == 0 else []
                tk = ctx.op('t', mm, waits=w8)
            stw = ctx.dtok('st%d' % par) if ('st%d' % par) in ctx.dsem else None
            ev = ctx.op('v', lambda v, i=i, t0j=t0j, tn=tn, ps=ps, ot=ot: v.tensor_scalar(
                ot[:, t0j:t0j + tn], ps[:, 0:tn], s_cx[:, 2:3], sbias[:, i:i + 1],
                ALU.mult, ALU.add), waits=[tk, e2b, stw])
            ev = ctx.op('v', lambda v, i=i, t0j=t0j, tn=tn, ot=ot: v.tensor_tensor(
                ot[:, t0j:t0j + tn], ot[:, t0j:t0j + tn],
                sres[:, i * Tn + t0j:i * Tn + t0j + tn], ALU.add), waits=[ev])
            prev_ev[bank] = ev
            evs.append(ev)
        ctx.dma('y', 'st%d' % par, out_o[i * 128:(i + 1) * 128, :], ot,
                waits=[evs[-1]], serial=True)
    ctx.emit()
    return nc


def run8(nc, in_maps):
    return run_bass_kernel_spmd(nc, in_maps, core_ids=list(range(NCORES)))


def asc(x):
    return np.ascontiguousarray(x)


def _cores():
    per_b = NCORES // B
    return [(c, c // per_b, (c % per_b) * T) for c in range(NCORES)]


def shard_cm(xg):
    return [asc(xg[b, :, t0:t0 + T]) for (_, b, t0) in _cores()]


def shard_tm(xg):
    return [asc(xg[b, t0:t0 + T, :]) for (_, b, t0) in _cores()]


def gather_cm(outs, C_):
    per_b = NCORES // B
    return np.stack([np.concatenate(outs[b * per_b:(b + 1) * per_b], axis=1)
                     for b in range(B)])


def gather_tm(outs, C_):
    per_b = NCORES // B
    return np.stack([np.concatenate(outs[b * per_b:(b + 1) * per_b], axis=0)
                     for b in range(B)])


def _run(nc, in_maps, out_names):
    res = run8(nc, in_maps)
    return [[np.asarray(r[nm]) for nm in out_names] for r in res.results]


def _forward(inputs, dbg=None):
    inp = {k: np.asarray(v, dtype=np.float32) if np.asarray(v).dtype == np.float32
           else np.asarray(v) for k, v in inputs.items()}
    nch_tot = L // CH0
    per_b = NCORES // B

    u = np.asarray(inp['L0_attn_u'], np.float32)
    uT = asc(u.transpose(0, 2, 1))
    W_inT = asc(np.asarray(inp['L0_attn_W_in'], np.float32).T)
    b_in = np.asarray(inp['L0_attn_b_in'], np.float32).reshape(-1, 1)

    nc = build_linear(DM, D_IN_PROJ, T, bias=True)
    outs = _run(nc, [{'x': xc, 'w': W_inT, 'bias': b_in} for xc in shard_cm(uT)], ['out'])
    zx = gather_cm([o[0] for o in outs], D_IN_PROJ)
    if dbg is not None:
        dbg['zx'] = zx

    xbc = zx[:, DSSM:DSSM + CONV_DIM, :]
    dt_raw = zx[:, DSSM + CONV_DIM:, :]
    conv_w = np.asarray(inp['L0_attn_conv_w'], np.float32)
    conv_b = np.asarray(inp['L0_attn_conv_b'], np.float32).reshape(-1, 1)
    dt_bias = np.asarray(inp['L0_attn_dt_bias'], np.float32).reshape(-1, 1)
    A_log = np.asarray(inp['L0_attn_A_log'], np.float32).reshape(-1, 1)
    xbc_pad_g = np.concatenate([np.zeros((B, CONV_DIM, 3), np.float32), xbc], axis=2)
    maps = []
    for (_, b, t0) in _cores():
        maps.append({'xbc_pad': asc(xbc_pad_g[b, :, t0:t0 + T + 3]),
                     'conv_w': conv_w, 'conv_b': conv_b,
                     'dt_raw': asc(dt_raw[b, :, t0:t0 + T]),
                     'dt_bias': dt_bias, 'A_log': A_log})
    nc = build_mamba_pre(Tn=T)
    outs = _run(nc, maps, ['convout', 'dt', 'cs', 'expcs', 'dsdt'])
    convout = gather_cm([o[0] for o in outs], CONV_DIM)
    dt_g = gather_cm([o[1] for o in outs], NH0)
    cs_g = gather_cm([o[2] for o in outs], NH0)
    expcs_g = gather_cm([o[3] for o in outs], NH0)
    dsdt_g = gather_cm([o[4] for o in outs], NH0)
    if dbg is not None:
        dbg.update(convout=convout, dt=dt_g, cs=cs_g, expcs=expcs_g, dsdt=dsdt_g)

    x_cm_g = convout[:, :DSSM, :]
    B_cm_g = convout[:, DSSM:DSSM + DST, :]
    C_cm_g = convout[:, DSSM + DST:, :]
    dt_tm_g = asc(dt_g.transpose(0, 2, 1))
    cs_tm_g = asc(cs_g.transpose(0, 2, 1))
    dsdt_tm_g = asc(dsdt_g.transpose(0, 2, 1))
    D_bc = np.broadcast_to(np.asarray(inp['L0_attn_D'], np.float32), (128, NH0)).copy()
    mask_ji = asc(np.tril(np.ones((CH0, CH0), np.float32)).T)
    ones1 = np.ones((1, 128), np.float32)
    ident = np.eye(128).astype(bf16)
    maps = []
    for (_, b, t0) in _cores():
        maps.append({'x_cm': asc(x_cm_g[b, :, t0:t0 + T]),
                     'B_cm': asc(B_cm_g[b, :, t0:t0 + T]),
                     'C_cm': asc(C_cm_g[b, :, t0:t0 + T]),
                     'dt_tm': asc(dt_tm_g[b, t0:t0 + T]),
                     'cs_row': asc(cs_g[b, :, t0:t0 + T]),
                     'cs_tm': asc(cs_tm_g[b, t0:t0 + T]),
                     'dsdt_tm': asc(dsdt_tm_g[b, t0:t0 + T]),
                     'D_bc': D_bc, 'mask': mask_ji, 'ones1': ones1, 'ident': ident})
    nc = build_ssd_local(Tn=T, ch=CH0)
    outs = _run(nc, maps, ['ydx', 'states'])
    ydx_g = gather_tm([o[0] for o in outs], DSSM)
    states_g = np.stack([np.concatenate(
        [outs[b * per_b + i][1].reshape(NCH0, DSSM, DST) for i in range(per_b)], axis=0)
        for b in range(B)])
    if dbg is not None:
        dbg['ydx'] = ydx_g
        dbg['states'] = states_g

    HQ = NH0 // (NCORES // B)
    rows = HQ * HD0
    decay = expcs_g[:, :, CH0 - 1::CH0]
    maps = []
    for c in range(NCORES):
        b, hq = c // per_b, c % per_b
        st = asc(states_g[b, :, hq * rows:(hq + 1) * rows, :])
        dcr = np.repeat(decay[b, hq * HQ:(hq + 1) * HQ, :].T, HD0, axis=1)
        dc = asc(dcr.reshape(nch_tot, rows // 128, 128).transpose(0, 2, 1))
        maps.append({'states': st, 'decay_bc': dc})
    nc = build_state_scan(nch_tot, rows)
    outs = _run(nc, maps, ['prev'])
    prev_g = np.stack([np.concatenate([outs[b * per_b + i][0] for i in range(per_b)],
                                      axis=1) for b in range(B)])
    prevT_g = asc(prev_g.transpose(0, 1, 3, 2))
    if dbg is not None:
        dbg['prev'] = prev_g

    expcs_tm_g = asc(expcs_g.transpose(0, 2, 1))
    z_tm_g = asc(zx[:, :DSSM, :].transpose(0, 2, 1))
    maps = []
    for (_, b, t0) in _cores():
        c0 = t0 // CH0
        maps.append({'ydx': asc(ydx_g[b, t0:t0 + T]),
                     'C_cm': asc(C_cm_g[b, :, t0:t0 + T]),
                     'prevT': asc(prevT_g[b, c0:c0 + NCH0].reshape(NCH0 * DST, DSSM)),
                     'expcs_tm': asc(expcs_tm_g[b, t0:t0 + T]),
                     'z_tm': asc(z_tm_g[b, t0:t0 + T])})
    nc = build_ssd_final(Tn=T, ch=CH0)
    outs = _run(nc, maps, ['yn'])
    yn_g = gather_tm([o[0] for o in outs], DSSM)
    if dbg is not None:
        dbg['yn'] = yn_g

    ynT = asc(yn_g.transpose(0, 2, 1))
    W_outT = asc(np.asarray(inp['L0_attn_W_out'], np.float32).T)
    b_out = np.asarray(inp['L0_attn_b_out'], np.float32).reshape(-1, 1)
    nw = np.asarray(inp['L0_attn_norm_w'], np.float32).reshape(-1, 1)
    maps = []
    for i, (_, b, t0) in enumerate(_cores()):
        maps.append({'x': asc(ynT[b, :, t0:t0 + T]), 'w': W_outT, 'bias': b_out,
                     'resid': asc(uT[b, :, t0:t0 + T]), 'wscale': nw})
    nc = build_linear(DSSM, DM, T, bias=True, residual=True, wscale=True,
                      x_hbm_dtype=BF16)
    outs = _run(nc, maps, ['out'])
    s1 = gather_cm([o[0] for o in outs], DM)
    if dbg is not None:
        dbg['s1'] = s1

    s1_tm = asc(s1.transpose(0, 2, 1))
    nc = build_ln(T, DM)
    outs = _run(nc, [{'x': xc} for xc in shard_tm(s1_tm)], ['out'])
    x1_tm = gather_tm([o[0] for o in outs], DM)
    if dbg is not None:
        dbg['x1'] = x1_tm

    x1T = asc(x1_tm.transpose(0, 2, 1))
    w1 = np.asarray(inp['L0_ffn_w1'], np.float32)
    b1 = np.asarray(inp['L0_ffn_b1'], np.float32).reshape(-1, 1)
    w2 = np.asarray(inp['L0_ffn_w2'], np.float32)
    b2 = np.asarray(inp['L0_ffn_b2'], np.float32).reshape(-1, 1)
    nc = build_ffn1(Tn=T)
    outs = _run(nc, [{'x': xc, 'w': w1, 'bias': b1} for xc in shard_cm(x1T)], ['out'])
    g_g = gather_cm([o[0] for o in outs], 4096)
    if dbg is not None:
        dbg['g'] = g_g

    nc = build_linear(2048, DM, T, bias=False, x_hbm_dtype=BF16)
    outs = _run(nc, [{'x': asc(g_g[b, :2048, t0:t0 + T]), 'w': asc(w2[:2048])}
                     for (_, b, t0) in _cores()], ['out'])
    part = gather_cm([o[0] for o in outs], DM)
    nc = build_f2b(Tn=T)
    outs = _run(nc, [{'x': asc(g_g[b, 2048:, t0:t0 + T]), 'w': asc(w2[2048:]),
                      'bias': b2, 'resid': asc(part[b, :, t0:t0 + T]),
                      'resid2': asc(x1T[b, :, t0:t0 + T])}
                     for (_, b, t0) in _cores()], ['out'])
    s2 = gather_cm([o[0] for o in outs], DM)
    s2_tm = asc(s2.transpose(0, 2, 1))
    nc = build_ln(T, DM)
    outs = _run(nc, [{'x': xc} for xc in shard_tm(s2_tm)], ['out'])
    x2_tm = gather_tm([o[0] for o in outs], DM)
    if dbg is not None:
        dbg['x2'] = x2_tm

    x2T = asc(x2_tm.transpose(0, 2, 1))
    Wq = np.asarray(inp['L1_attn_Wq'], np.float32)
    Wk = np.asarray(inp['L1_attn_Wk'], np.float32)
    Wv = np.asarray(inp['L1_attn_Wv'], np.float32)
    Wa = np.asarray(inp['L1_attn_Wa'], np.float32)
    Wb = np.asarray(inp['L1_attn_Wb'], np.float32)
    Wg = np.asarray(inp['L1_attn_Wg'], np.float32)
    Wcat = asc(np.concatenate([Wq, Wk, Wv, Wa, Wb, Wg], axis=1))
    nc = build_linear(DM, 6160, T)
    outs = _run(nc, [{'x': xc, 'w': Wcat} for xc in shard_cm(x2T)], ['out'])
    proj = gather_cm([o[0] for o in outs], 6160)
    q_raw = proj[:, 0:1024, :]
    k_raw = proj[:, 1024:2048, :]
    v_raw = proj[:, 2048:4096, :]
    a_raw = proj[:, 4096:4104, :]
    b_raw = proj[:, 4104:4112, :]
    g_raw = proj[:, 4112:6160, :]
    if dbg is not None:
        dbg['proj'] = proj

    convs = {}
    for nm, arr, wkey in [('q', q_raw, 'L1_attn_conv_q'), ('k', k_raw, 'L1_attn_conv_k'),
                          ('v', v_raw, 'L1_attn_conv_v')]:
        C_ = arr.shape[1]
        pad = np.concatenate([np.zeros((B, C_, 3), np.float32), arr], axis=2)
        wc = np.asarray(inp[wkey], np.float32)
        nc = build_conv_silu(C_, Tn=T)
        outs = _run(nc, [{'x_pad': asc(pad[b, :, t0:t0 + T + 3]), 'w': wc}
                         for (_, b, t0) in _cores()], ['out'])
        convs[nm] = gather_cm([o[0] for o in outs], C_)
    if dbg is not None:
        dbg.update(qc=convs['q'], kc=convs['k'], vc=convs['v'])

    q_tm = asc(convs['q'].transpose(0, 2, 1))
    k_tm = asc(convs['k'].transpose(0, 2, 1))
    nc = build_l2norm_qk(Tn=T)
    outs = _run(nc, [{'q_tm': asc(q_tm[b, t0:t0 + T]), 'k_tm': asc(k_tm[b, t0:t0 + T])}
                     for (_, b, t0) in _cores()], ['qn', 'kn'])
    qn_g = gather_tm([o[0] for o in outs], KEY_DIM)
    kn_g = gather_tm([o[1] for o in outs], KEY_DIM)
    if dbg is not None:
        dbg.update(qn=qn_g, kn=kn_g)

    dtb1 = np.asarray(inp['L1_attn_dt_bias'], np.float32).reshape(-1, 1)
    alog1 = np.asarray(inp['L1_attn_A_log'], np.float32).reshape(-1, 1)
    nc = build_ab_rows(Tn=T, ch=CH1)
    outs = _run(nc, [{'a_raw': asc(a_raw[b, :, t0:t0 + T]),
                      'b_raw': asc(b_raw[b, :, t0:t0 + T]),
                      'dt_bias': dtb1, 'A_log': alog1} for (_, b, t0) in _cores()],
                ['lg', 'beta'])
    lg_g = gather_cm([o[0] for o in outs], H1)
    beta_g = gather_cm([o[1] for o in outs], H1)
    if dbg is not None:
        dbg.update(lg=lg_g, beta=beta_g)

    qn_cm = asc(qn_g.transpose(0, 2, 1))
    kn_cm = asc(kn_g.transpose(0, 2, 1))
    v_tm_g = asc(convs['v'].transpose(0, 2, 1))
    lg_tm_g = asc(lg_g.transpose(0, 2, 1))
    beta_tm_g = asc(beta_g.transpose(0, 2, 1))
    identF = np.eye(128, dtype=np.float32)
    ident2 = (2 * np.eye(CH1)).astype(np.float32)
    mask_incl = asc(np.tril(np.ones((CH1, CH1), np.float32)).T)
    mask_strict = asc(np.tril(np.ones((CH1, CH1), np.float32), -1).T)
    ones1 = np.ones((1, 128), np.float32)
    nc = build_delta_local(Tn=T, ch=CH1)
    maps = []
    for (_, b, t0) in _cores():
        maps.append({'q_cm': asc(qn_cm[b, :, t0:t0 + T]), 'k_cm': asc(kn_cm[b, :, t0:t0 + T]),
                     'q_tm': asc(qn_g[b, t0:t0 + T]), 'k_tm': asc(kn_g[b, t0:t0 + T]),
                     'v_tm': asc(v_tm_g[b, t0:t0 + T]),
                     'lg_row': asc(lg_g[b, :, t0:t0 + T]),
                     'lg_tm': asc(lg_tm_g[b, t0:t0 + T]),
                     'beta_tm': asc(beta_tm_g[b, t0:t0 + T]),
                     'identF': identF, 'ident2': ident2, 'mask_incl': mask_incl,
                     'mask_strict': mask_strict, 'ones1': ones1})
    outs = _run(nc, maps, ['O_loc', 'Mt', 'SlocT', 'GT', 'gamc'])
    O_loc_g = gather_tm([o[0] for o in outs], VAL_DIM)
    nchg = L // CH1
    Mt_g = np.stack([np.concatenate(
        [outs[b * per_b + i][1].reshape(NCH1, H1, KD1, KD1) for i in range(per_b)], axis=0)
        for b in range(B)])
    Sl_g = np.stack([np.concatenate(
        [outs[b * per_b + i][2].reshape(NCH1, H1, KD1, VD1) for i in range(per_b)], axis=0)
        for b in range(B)])
    GT_g = np.stack([np.concatenate(
        [outs[b * per_b + i][3].reshape(NCH1, H1, KD1, CH1) for i in range(per_b)], axis=0)
        for b in range(B)])
    gamc_g = np.stack([np.concatenate(
        [outs[b * per_b + i][4] for i in range(per_b)], axis=0) for b in range(B)])
    if dbg is not None:
        dbg.update(O_loc=O_loc_g, Mt=Mt_g, Sl=Sl_g, GT=GT_g, gamc=gamc_g)

    units = [(b, h) for b in range(B) for h in range(H1)]
    upc = len(units) // NCORES
    maps = []
    for c in range(NCORES):
        us = units[c * upc:(c + 1) * upc]
        maps.append({
            'Mt': asc(np.concatenate([Mt_g[b, :, h] for (b, h) in us], axis=0)
                      ).reshape(upc * nchg * KD1, KD1),
            'SlocT': asc(np.concatenate([Sl_g[b, :, h] for (b, h) in us], axis=0)
                         ).reshape(upc * nchg * KD1, VD1),
            'GT': asc(np.concatenate([GT_g[b, :, h] for (b, h) in us], axis=0)
                      ).reshape(upc * nchg * KD1, CH1),
            'gam_bc': asc(np.broadcast_to(
                np.concatenate([gamc_g[b, :, h] for (b, h) in us])[:, None],
                (upc * nchg, 128)).copy()),
        })
    nc = build_delta_chain(upc, nchg, ch=CH1)
    outs = _run(nc, maps, ['O_corr'])
    O_corr_g = np.zeros((B, L, VAL_DIM), np.float32)
    for c in range(NCORES):
        oc = outs[c][0].reshape(upc, nchg, CH1, VD1)
        for iu, (b, h) in enumerate(units[c * upc:(c + 1) * upc]):
            O_corr_g[b, :, h * VD1:(h + 1) * VD1] = oc[iu].reshape(L, VD1)
    if dbg is not None:
        dbg['O_corr'] = O_corr_g

    g_tm_g = asc(g_raw.transpose(0, 2, 1))
    nc = build_gate_out(Tn=T)
    outs = _run(nc, [{'O_loc': asc(O_loc_g[b, t0:t0 + T]),
                      'O_corr': asc(O_corr_g[b, t0:t0 + T]),
                      'g_tm': asc(g_tm_g[b, t0:t0 + T])} for (_, b, t0) in _cores()],
                ['og'])
    og_g = gather_tm([o[0] for o in outs], VAL_DIM)
    if dbg is not None:
        dbg['og'] = og_g

    ogT = asc(og_g.transpose(0, 2, 1))
    WoT = asc(np.asarray(inp['L1_attn_Wo'], np.float32))
    onw = np.tile(np.asarray(inp['L1_attn_o_norm_w'], np.float32), H1).reshape(-1, 1)
    nc = build_linear(VAL_DIM, DM, T, residual=True, wscale=True, x_hbm_dtype=BF16)
    outs = _run(nc, [{'x': asc(ogT[b, :, t0:t0 + T]), 'w': WoT, 'wscale': onw,
                      'resid': asc(x2T[b, :, t0:t0 + T])} for (_, b, t0) in _cores()],
                ['out'])
    s3 = gather_cm([o[0] for o in outs], DM)
    s3_tm = asc(s3.transpose(0, 2, 1))
    nc = build_ln(T, DM)
    outs = _run(nc, [{'x': xc} for xc in shard_tm(s3_tm)], ['out'])
    x3_tm = gather_tm([o[0] for o in outs], DM)
    if dbg is not None:
        dbg['x3'] = x3_tm

    x3T = asc(x3_tm.transpose(0, 2, 1))
    nc = build_amax_part(DM, Tn=T)
    outs = _run(nc, [{'x': asc(x3T[b, :, t0:t0 + T]), 'identF': identF}
                     for (_, b, t0) in _cores()], ['part'])
    parts = asc(np.stack([o[0][0, 0] for o in outs])[None, :])
    w1f = asc(np.asarray(inp['L1_ffn_weight'], np.float32).T)
    b1f = np.asarray(inp['L1_ffn_bias'], np.float32).reshape(-1, 1)
    nc = build_ffn_fp8(Tn=T)
    outs = _run(nc, [{'parts': parts, 'x': asc(x3T[b, :, t0:t0 + T]), 'w': w1f,
                      'bias': b1f, 'resid': asc(x3T[b, :, t0:t0 + T]),
                      'ones1': ones1, 'identF': identF} for (_, b, t0) in _cores()],
                ['out'])
    s4 = gather_cm([o[0] for o in outs], DM)
    s4_tm = asc(s4.transpose(0, 2, 1))
    nc = build_ln(T, DM)
    outs = _run(nc, [{'x': xc} for xc in shard_tm(s4_tm)], ['out'])
    x4_tm = gather_tm([o[0] for o in outs], DM)
    return x4_tm


def kernel(**inputs):
    return _forward(inputs)
